# version 02
# baseline (speedup 1.0000x reference)
import math
import numpy as np
import ml_dtypes
import concourse.bass as bass
import concourse.mybir as mybir
from concourse.bass_utils import run_bass_kernel_spmd

F32 = mybir.dt.float32
BF16 = mybir.dt.bfloat16
AF = mybir.ActivationFunctionType
ALU = mybir.AluOpType
AX = mybir.AxisListType

B, T, D, NH, HS, NE, DI = 4, 4096, 2048, 16, 128, 8, 1024
TLOC = 2048
NFC = D // 128
SCALE = 1.0 / math.sqrt(HS)
AMGM = 0.5 * SCALE
EPS = 1e-5
KBCNT = (8, 16, 24, 32)
G_OF_HALF = ((0, 3, 4, 7), (1, 2, 5, 6))

ENGS = ("sync", "gpsimd", "tensor", "vector", "scalar")
SEMS = ("sSY", "sGP", "sSC", "sMM", "sDVE", "sACT")


def _flat(waits):
    out = []
    for w in waits:
        if w is None:
            continue
        if isinstance(w, list):
            out.extend(x for x in w if x is not None)
        else:
            out.append(w)
    return tuple(out)


class Emit:
    def __init__(self):
        self.ops = {e: [] for e in ENGS}
        self.cnt = {s: 0 for s in SEMS}
        self.last_inc = {e: True for e in ENGS}

    def op(self, eng, fn, waits=(), inc=None, amt=1):
        tok = None
        if inc is not None:
            self.cnt[inc] += amt
            tok = (inc, self.cnt[inc])
        self.ops[eng].append((fn, _flat(waits), (inc, amt) if inc else None))
        self.last_inc[eng] = inc is not None
        return tok

    def dma(self, queue, out_fn, in_fn, waits=()):
        sem = {"sync": "sSY", "gpsimd": "sGP", "scalar": "sSC"}[queue]
        def fn(e, o=out_fn, i=in_fn):
            return e.dma_start(o(), i())
        return self.op(queue, fn, waits=waits, inc=sem, amt=16)

    def barrier(self):
        if not self.last_inc["tensor"]:
            self.op("tensor", lambda e: e.nop(), inc="sMM")
        if not self.last_inc["vector"]:
            self.op("vector", self.dummy_vec, inc="sDVE")
        if not self.last_inc["scalar"]:
            self.op("scalar", self.dummy_act, inc="sACT")
        snap = tuple((s, v) for s, v in self.cnt.items() if v > 0)
        for e in ENGS:
            self.ops[e].append((None, snap, None))


def materialize(nc, em):
    import contextlib
    stk = contextlib.ExitStack()
    sems = {s: stk.enter_context(nc.semaphore(s)) for s in SEMS}
    with nc.Block() as block:
        for eng in ENGS:
            ops = em.ops[eng]

            def mk(ops=ops):
                def f(e):
                    hw = {s: 0 for s in SEMS}
                    for fn, waits, inc in ops:
                        for (s, v) in waits:
                            if v > hw[s]:
                                e.wait_ge(sems[s], v)
                                hw[s] = v
                        if fn is None:
                            continue
                        inst = fn(e)
                        if inc:
                            inst.then_inc(sems[inc[0]], inc[1])
                return f
            getattr(block, eng)(mk())
    return stk


DBG_STAGE = None


def build():
    dbg = DBG_STAGE
    nc = bass.Bass("TRN2", target_bir_lowering=False)
    em = Emit()

    din = {}
    def I(name, shape, dt=F32):
        din[name] = nc.dram_tensor(name, list(shape), dt, kind="ExternalInput")
        return din[name]

    xt_in = I("xt", (D, T))
    xtq_in = I("xtq", (D, TLOC))
    xres_in = I("xres", (TLOC, D))
    wq_in = I("wq", (D, D)); wk_in = I("wk", (D, D))
    wv_in = I("wv", (D, D)); wo_in = I("wo", (D, D))
    bq_in = I("bq", (1, D)); bk_in = I("bk", (1, D))
    bv_in = I("bv", (1, D)); bo_in = I("bo", (1, D))
    ckf_in = I("ckf", (128, T)); skf_in = I("skf", (128, T))
    cqf_in = I("cqf", (128, TLOC)); sqf_in = I("sqf", (128, TLOC))
    swap_in = I("swp", (128, 128))
    id32_in = I("id32", (128, 128))
    masks_in = I("masks", (4, 8, 128, 512), BF16)
    rgt_in = I("rgt", (D, NE))
    wg_in = I("wg", (NE, D, DI)); wu_in = I("wu", (NE, D, DI))
    wd_in = I("wd", (NE, DI, D))

    out_d = nc.dram_tensor("out", [TLOC, D], F32, kind="ExternalOutput")

    k1 = "ExternalOutput" if dbg == "s1" else "Internal"
    k3 = "ExternalOutput" if dbg == "s3" else "Internal"
    qT_d = nc.dram_tensor("qT_d", [NH, 128, TLOC], BF16, kind=k1)
    kT_d = nc.dram_tensor("kT_d", [NH, 128, T], BF16, kind=k1)
    vT_d = nc.dram_tensor("vT_d", [NH, 128, T], BF16, kind=k1)
    x1_d = nc.dram_tensor("x1_d", [TLOC, D], F32, kind=k3)
    x1h_d = nc.dram_tensor("x1h_d", [NFC, 128, TLOC], BF16, kind=k3)
    x1l_d = nc.dram_tensor("x1l_d", [NFC, 128, TLOC], BF16, kind=k3)
    if dbg == "s2":
        oT_dbg = nc.dram_tensor("oT_dbg", [NH, 128, TLOC], BF16, kind="ExternalOutput")

    import contextlib
    pstk = contextlib.ExitStack()
    psf = [pstk.enter_context(nc.psum_tensor(f"psf{i}", [128, 512], F32)) for i in range(7)]
    psb = pstk.enter_context(nc.psum_tensor("psb", [128, 1024], BF16))
    bank_free = {i: None for i in range(7)}
    bankb_free = [None]

    ar0 = nc.auto_arena()
    ones512 = ar0.sbuf_tensor("ones512", [1, 512], F32)
    ones1b = ar0.sbuf_tensor("ones1b", [128, 1], BF16)
    idbf = ar0.sbuf_tensor("idbfs", [128, 128], BF16)
    swp = ar0.sbuf_tensor("swps", [128, 128], F32)
    dum0 = ar0.sbuf_tensor("dum0", [1, 16], F32)
    eps_c = ar0.sbuf_tensor("eps_c", [128, 1], F32)
    ar0.allocate()
    em.dummy_vec = lambda e: e.memset(dum0[:, :], 0.0)
    em.dummy_act = lambda e: e.copy(dum0[0:1, 0:16], dum0[0:1, 0:16])
    t_eps = em.op("vector", lambda e: e.memset(eps_c[:, :], EPS), inc="sDVE")

    t_ones = em.op("vector", lambda e: e.memset(ones512[:, :], 1.0), inc="sDVE")
    t_ones1b = em.op("vector", lambda e: e.memset(ones1b[:, :], 1.0), inc="sDVE")
    t_idb = em.dma("gpsimd", lambda: idbf[:, :], lambda: id32_in[:, :])
    t_swp = em.dma("sync", lambda: swp[:, :], lambda: swap_in[:, :])

    ar1 = nc.auto_arena()
    xt_sb = ar1.sbuf_tensor("xt_sb", [128, NFC * T], BF16)
    csk = ar1.sbuf_tensor("csk", [128, T], BF16)
    snk = ar1.sbuf_tensor("snk", [128, T], BF16)
    bk_s = ar1.sbuf_tensor("bk_s", [1, D], F32)
    bv_s = ar1.sbuf_tensor("bv_s", [1, D], F32)
    w1_sb = [ar1.sbuf_tensor(f"w1_{i}", [128, NFC * 128], BF16) for i in range(2)]
    qsb1 = [ar1.sbuf_tensor(f"qsb1_{i}", [128, 512], F32) for i in range(2)]
    ta1 = [ar1.sbuf_tensor(f"ta1_{i}", [128, 512], F32) for i in range(2)]
    tb1 = [ar1.sbuf_tensor(f"tb1_{i}", [128, 512], F32) for i in range(2)]
    stg1 = [ar1.sbuf_tensor(f"stg1_{i}", [128, 512], BF16) for i in range(4)]
    ar1.allocate()

    t_xt = []
    for c in range(NFC):
        t_xt.append(em.dma("gpsimd", lambda c=c: xt_sb[:, c * T:(c + 1) * T],
                           lambda c=c: xt_in[c * 128:(c + 1) * 128, :]))
    t_cs = em.dma("gpsimd", lambda: csk[:, :], lambda: ckf_in[:, :])
    t_sn = em.dma("gpsimd", lambda: snk[:, :], lambda: skf_in[:, :])
    t_bk = em.dma("sync", lambda: bk_s[:, :], lambda: bk_in[:, :])
    t_bv = em.dma("sync", lambda: bv_s[:, :], lambda: bv_in[:, :])

    def w_colchunk_dma(w_in, dc, dst, waits):
        def src(w_in=w_in, dc=dc):
            return w_in[:, dc * 128:(dc + 1) * 128].rearrange("(c p) d -> p c d", p=128)
        def dstf(dst=dst):
            return dst[:, :].rearrange("p (c d) -> p c d", c=NFC)
        return em.dma("gpsimd", dstf, src, waits=waits)

    def proj_rope(w_in, b_s, t_b, x_sb, x_tok, t_x_last, cs_sb, sn_sb, t_cst, t_snt,
                  dst_d, n_tg, w_bufs, qsb, ta, tb, stg, pst):
        w_tok = pst["w"]
        w_guard = pst["wg"]
        qs_tok = [None, None]
        ta_tok = [None, None]
        tb_tok = [None, None]
        st_tok = pst["st"]
        pending = [None]
        for dc in range(NH):
            wb = dc % 2
            w_tok[wb] = w_colchunk_dma(w_in, dc, w_bufs[wb], waits=[w_tok[wb], w_guard[wb]])
            for tg in range(n_tg):
                i = (dc * n_tg + tg) % 2
                ps = psf[i]; psw = psf[2 + i]
                em.op("tensor", lambda e, ps=ps, dc=dc, b_s=b_s: e.matmul(
                    ps[:, :], b_s[0:1, dc * 128:(dc + 1) * 128], ones512[0:1, :],
                    start=True, stop=False),
                    waits=[bank_free[i], w_tok[wb], t_ones, t_b, t_x_last])
                tk = None
                for f in range(NFC):
                    last = f == NFC - 1
                    tk = em.op("tensor", lambda e, ps=ps, f=f, tg=tg, wb=wb, x_sb=x_sb,
                               x_tok=x_tok, w_bufs=w_bufs: e.matmul(
                        ps[:, :], w_bufs[wb][:, f * 128:(f + 1) * 128],
                        x_sb[:, f * x_tok + tg * 512: f * x_tok + tg * 512 + 512],
                        start=False, stop=last), inc="sMM" if last else None)
                w_guard[wb] = tk
                qs_tok[i] = em.op("vector", lambda e, qs=qsb[i], ps=ps: e.tensor_copy(
                    qs[:, :], ps[:, :]), waits=[tk, qs_tok[i]], inc="sDVE")
                bank_free[i] = qs_tok[i]

                def mk_pending(i=i, dc=dc, tg=tg, psw=psw, qs_t=qs_tok[i]):
                    def go():
                        tks = em.op("tensor", lambda e, psw=psw, qs=qsb[i]: e.matmul(
                            psw[:, :], swp[:, :], qs[:, :], start=True, stop=True),
                            waits=[qs_t, bank_free[2 + i], t_swp], inc="sMM")
                        col = tg * 512
                        tka = em.op("vector", lambda e, t_=ta[i], qs=qsb[i], col=col, cs=cs_sb:
                                    e.tensor_mul(t_[:, :], qs[:, :], cs[:, col:col + 512]),
                                    waits=[qs_t, t_cst, ta_tok[i]], inc="sDVE")
                        tkb = em.op("vector", lambda e, t_=tb[i], psw=psw, col=col, sn=sn_sb:
                                    e.tensor_mul(t_[:, :], psw[:, :], sn[:, col:col + 512]),
                                    waits=[tks, t_snt, tb_tok[i]], inc="sDVE")
                        bank_free[2 + i] = tkb
                        si = (dc * n_tg + tg) % 4
                        tke = em.op("vector", lambda e, st=stg[si], a=ta[i], b_=tb[i]:
                                    e.tensor_add(st[:, :], a[:, :], b_[:, :]),
                                    waits=[tka, tkb, st_tok[si]], inc="sDVE")
                        ta_tok[i] = tke
                        tb_tok[i] = tke
                        st_tok[si] = em.dma("scalar",
                                            lambda dc=dc, tg=tg, dst_d=dst_d:
                                            dst_d[dc, :, tg * 512:(tg + 1) * 512],
                                            lambda si=si, stg=stg: stg[si][:, :], waits=[tke])
                    return go
                if pending[0] is not None:
                    pending[0]()
                pending[0] = mk_pending()
        pending[0]()
        pending[0] = None
        return st_tok

    def proj_plain(w_in, b_s, t_b, x_sb, x_tok, t_x_last, dst_d, n_tg, w_bufs, stg, pst):
        w_tok = pst["w"]
        w_guard = pst["wg"]
        st_tok = pst["st"]
        for dc in range(NH):
            wb = dc % 2
            w_tok[wb] = w_colchunk_dma(w_in, dc, w_bufs[wb], waits=[w_tok[wb], w_guard[wb]])
            for tg in range(n_tg):
                i = (dc * n_tg + tg) % 2
                ps = psf[i]
                em.op("tensor", lambda e, ps=ps, dc=dc, b_s=b_s: e.matmul(
                    ps[:, :], b_s[0:1, dc * 128:(dc + 1) * 128], ones512[0:1, :],
                    start=True, stop=False),
                    waits=[bank_free[i], w_tok[wb], t_ones, t_b, t_x_last])
                tk = None
                for f in range(NFC):
                    last = f == NFC - 1
                    tk = em.op("tensor", lambda e, ps=ps, f=f, tg=tg, wb=wb: e.matmul(
                        ps[:, :], w_bufs[wb][:, f * 128:(f + 1) * 128],
                        x_sb[:, f * x_tok + tg * 512: f * x_tok + tg * 512 + 512],
                        start=False, stop=last), inc="sMM" if last else None)
                w_guard[wb] = tk
                si = (dc * n_tg + tg) % 4
                tke = em.op("vector", lambda e, st=stg[si], ps=ps: e.tensor_copy(
                    st[:, :], ps[:, :]), waits=[tk, st_tok[si]], inc="sDVE")
                bank_free[i] = tke
                st_tok[si] = em.dma("scalar",
                                    lambda dc=dc, tg=tg: vT_d[dc, :, tg * 512:(tg + 1) * 512],
                                    lambda si=si, stg=stg: stg[si][:, :], waits=[tke])

    pst1 = {"w": [None, None], "wg": [None, None], "st": [None] * 4}
    proj_rope(wk_in, bk_s, t_bk, xt_sb, T, t_xt[-1], csk, snk, t_cs, t_sn,
              kT_d, 8, w1_sb, qsb1, ta1, tb1, stg1, pst1)
    proj_plain(wv_in, bv_s, t_bv, xt_sb, T, t_xt[-1], vT_d, 8, w1_sb, stg1, pst1)
    em.barrier()
    ar1.free()

    ar1b = nc.auto_arena()
    xtq_sb = ar1b.sbuf_tensor("xtq_sb", [128, NFC * TLOC], BF16)
    csq = ar1b.sbuf_tensor("csq", [128, TLOC], BF16)
    snq = ar1b.sbuf_tensor("snq", [128, TLOC], BF16)
    bq_s = ar1b.sbuf_tensor("bq_s", [1, D], F32)
    w1b_sb = [ar1b.sbuf_tensor(f"w1b_{i}", [128, NFC * 128], BF16) for i in range(2)]
    qsb1b = [ar1b.sbuf_tensor(f"qsb1b_{i}", [128, 512], F32) for i in range(2)]
    ta1b = [ar1b.sbuf_tensor(f"ta1b_{i}", [128, 512], F32) for i in range(2)]
    tb1b = [ar1b.sbuf_tensor(f"tb1b_{i}", [128, 512], F32) for i in range(2)]
    stg1b = [ar1b.sbuf_tensor(f"stg1b_{i}", [128, 512], BF16) for i in range(4)]
    ar1b.allocate()

    t_xtq = []
    for c in range(NFC):
        t_xtq.append(em.dma("gpsimd", lambda c=c: xtq_sb[:, c * TLOC:(c + 1) * TLOC],
                            lambda c=c: xtq_in[c * 128:(c + 1) * 128, :]))
    t_csq = em.dma("gpsimd", lambda: csq[:, :], lambda: cqf_in[:, :])
    t_snq = em.dma("gpsimd", lambda: snq[:, :], lambda: sqf_in[:, :])
    t_bq = em.dma("sync", lambda: bq_s[:, :], lambda: bq_in[:, :])
    pst1b = {"w": [None, None], "wg": [None, None], "st": [None] * 4}
    proj_rope(wq_in, bq_s, t_bq, xtq_sb, TLOC, t_xtq[-1], csq, snq, t_csq, t_snq,
              qT_d, 4, w1b_sb, qsb1b, ta1b, tb1b, stg1b, pst1b)
    em.barrier()
    ar1b.free()
    if dbg == "s1":
        em.op("scalar", lambda e: e.nop(), waits=[("sSC", em.cnt["sSC"])], inc="sACT")
        return nc, em

    ar2o = nc.auto_arena()
    oT_sb = ar2o.sbuf_tensor("oT_sb", [128, NH * TLOC], BF16)
    ar2o.allocate()
    ar2 = nc.auto_arena()
    q_h = [ar2.sbuf_tensor(f"q_h{i}", [128, TLOC], BF16) for i in range(2)]
    k_h = [ar2.sbuf_tensor(f"k_h{i}", [128, T], BF16) for i in range(2)]
    vT_h = [ar2.sbuf_tensor(f"vT_h{i}", [128, T], BF16) for i in range(2)]
    v_tm = [ar2.sbuf_tensor(f"v_tm{i}", [128, T], BF16) for i in range(2)]
    sqk = ar2.sbuf_tensor("sqk", [128, T], BF16)
    negk2 = [ar2.sbuf_tensor(f"negk2_{i}", [128, 32], F32) for i in range(2)]
    expk2f = [ar2.sbuf_tensor(f"expk2f_{i}", [128, 32], F32) for i in range(2)]
    expk2b = [ar2.sbuf_tensor(f"expk2b_{i}", [128, 32], BF16) for i in range(2)]
    ptile = [ar2.sbuf_tensor(f"ptile{i}", [128, 512], BF16) for i in range(4)]
    ptm = [ar2.sbuf_tensor(f"ptm{i}", [128, 512], BF16) for i in range(2)]
    rzrow = ar2.sbuf_tensor("rzrow", [1, 512], F32)
    rzbc = ar2.sbuf_tensor("rzbc", [128, 512], F32)
    masks_sb = ar2.sbuf_tensor("masks_sb", [128, 4 * 8 * 512], BF16)
    ar2.allocate()

    t_masks = None
    for l in range(4):
        for kk in range(8):
            t_masks = em.dma("sync",
                             lambda l=l, kk=kk: masks_sb[:, (l * 8 + kk) * 512:(l * 8 + kk + 1) * 512],
                             lambda l=l, kk=kk: masks_in[l, kk, :, :])

    t_q = [None, None]; t_k = [None, None]; t_vt = [None, None]
    vtm_tok = [None, None]
    sqk_tok = None
    nk_tok = [None, None]; ek_tok = [None, None]; eb_tok = [None, None]
    pt_tok = [None] * 4; ptm_tok = [None, None]
    rz_tok = None; rzbc_tok = None; rz_reader = None
    g_pe = [None, None]
    g_act = [None, None]

    for h in range(NH):
        hb = h % 2
        t_q[hb] = em.dma("sync", lambda hb=hb: q_h[hb][:, :], lambda h=h: qT_d[h, :, :],
                         waits=[t_q[hb], g_pe[hb]])
        t_k[hb] = em.dma("sync", lambda hb=hb: k_h[hb][:, :], lambda h=h: kT_d[h, :, :],
                         waits=[t_k[hb], g_pe[hb]])
        t_vt[hb] = em.dma("sync", lambda hb=hb: vT_h[hb][:, :], lambda h=h: vT_d[h, :, :],
                          waits=[t_vt[hb], g_pe[hb]])
        tk_sq = em.op("vector", lambda e, hb=hb: e.tensor_mul(
            sqk[:, :], k_h[hb][:, :], k_h[hb][:, :]), waits=[t_k[hb], sqk_tok], inc="sDVE")
        kn = psf[4]
        tk_kn = None
        for kb in range(32):
            tk_kn = em.op("tensor", lambda e, kn=kn, kb=kb: e.matmul(
                kn[:, kb:kb + 1], sqk[:, kb * 128:(kb + 1) * 128], ones1b[:, :],
                start=(kb == 0), stop=(kb == 31)),
                waits=[tk_sq, t_ones1b, bank_free[4]] if kb == 0 else [],
                inc="sMM" if kb == 31 else None)
        sqk_tok = tk_kn
        nk_tok[hb] = em.op("vector", lambda e, hb=hb, kn=kn: e.tensor_scalar_mul(
            negk2[hb][:, 0:32], kn[:, 0:32], -AMGM),
            waits=[tk_kn, nk_tok[hb], g_act[hb]], inc="sDVE")
        ek_tok[hb] = em.op("scalar", lambda e, hb=hb, kn=kn: e.activation(
            expk2f[hb][:, 0:32], kn[:, 0:32], AF.Exp, bias=0.0, scale=AMGM),
            waits=[tk_kn, ek_tok[hb]], inc="sACT")
        bank_free[4] = [nk_tok[hb], ek_tok[hb]]
        eb_tok[hb] = em.op("vector", lambda e, hb=hb: e.tensor_copy(
            expk2b[hb][:, :], expk2f[hb][:, :]),
            waits=[ek_tok[hb], eb_tok[hb], g_pe[hb]], inc="sDVE")
        for kb in range(32):
            tk_tr = em.op("tensor", lambda e, hb=hb, kb=kb: e.transpose(
                psb[:, 0:128], vT_h[hb][:, kb * 128:(kb + 1) * 128], idbf[:, :]),
                waits=[t_vt[hb], t_idb, bankb_free[0]], inc="sMM")
            vtm_tok[hb] = em.op("vector", lambda e, hb=hb, kb=kb: e.tensor_scalar(
                v_tm[hb][:, kb * 128:(kb + 1) * 128], psb[:, 0:128],
                expk2f[hb][:, kb:kb + 1], None, ALU.mult),
                waits=[tk_tr, ek_tok[hb], vtm_tok[hb]], inc="sDVE")
            bankb_free[0] = vtm_tok[hb]
        last_te = None
        t_bc = None
        SBANKS = (0, 1, 5, 6)
        DL = 4
        s_tok4 = [None] * 4
        for l in range(4):
            ql = l * 512
            cnt = KBCNT[l]
            ob = 2 + ((h * 4 + l) % 2)
            po = psf[ob]
            pz = psf[4]
            t_omm = None; t_zmm = None
            for i in range(32 + DL):
                j = i - DL
                if j >= 0:
                    sb = SBANKS[j % 4]
                    ps = psf[sb]
                    pp = j % 4
                    t_e = em.op("scalar", lambda e, ps=ps, hb=hb, kb=j, pp=pp: e.activation(
                        ptile[pp][:, :], ps[:, :], AF.Exp,
                        bias=negk2[hb][:, kb:kb + 1], scale=SCALE),
                        waits=[s_tok4[j % 4], nk_tok[hb], pt_tok[pp]], inc="sACT")
                    last_te = t_e
                    bank_free[sb] = t_e
                    t_zmm = em.op("tensor", lambda e, pz=pz, hb=hb, pp=pp, kb=j: e.matmul(
                        pz[0:1, :], expk2b[hb][:, kb:kb + 1], ptile[pp][:, :],
                        start=(kb == 0), stop=(kb == 31)),
                        waits=[t_e, eb_tok[hb]] + ([bank_free[4]] if j == 0 else []),
                        inc="sMM" if j == 31 else None)
                    if j < cnt:
                        msk = j - (cnt - 8)
                        if msk >= 0:
                            mp = j % 2
                            t_m = em.op("vector", lambda e, pp=pp, l=l, msk=msk, mp=mp: e.tensor_mul(
                                ptm[mp][:, :], ptile[pp][:, :],
                                masks_sb[:, (l * 8 + msk) * 512:(l * 8 + msk + 1) * 512]),
                                waits=[t_e, t_masks, ptm_tok[mp]], inc="sDVE")
                            src_t = ptm[mp]; wsrc = t_m
                        else:
                            src_t = ptile[pp]; wsrc = t_e
                        t_omm = em.op("tensor", lambda e, po=po, hb=hb, kb=j, src_t=src_t: e.matmul(
                            po[:, :], v_tm[hb][:, kb * 128:(kb + 1) * 128], src_t[:, :],
                            start=(kb == 0), stop=(kb == cnt - 1)),
                            waits=[wsrc, vtm_tok[hb]] + ([bank_free[ob]] if j == 0 else []),
                            inc="sMM" if j == cnt - 1 else None)
                        if msk >= 0:
                            ptm_tok[j % 2] = t_omm
                        pt_tok[pp] = t_omm
                    else:
                        pt_tok[pp] = t_zmm
                if i < 32:
                    sb = SBANKS[i % 4]
                    ps = psf[sb]
                    s_tok4[i % 4] = em.op("tensor", lambda e, ps=ps, hb=hb, kb=i, ql=ql: e.matmul(
                        ps[:, :], k_h[hb][:, kb * 128:(kb + 1) * 128], q_h[hb][:, ql:ql + 512],
                        start=True, stop=True),
                        waits=[t_q[hb], t_k[hb], bank_free[sb]], inc="sMM")
            rz_tok = em.op("vector", lambda e, pz=pz: e.reciprocal(rzrow[0:1, :], pz[0:1, :]),
                           waits=[t_zmm, rz_tok, rz_reader], inc="sDVE")
            t_bc = em.op("tensor", lambda e, pz=pz: e.matmul(
                pz[0:128, :], ones512[0:1, 0:128], rzrow[0:1, :], start=True, stop=True),
                waits=[rz_tok], inc="sMM")
            rz_reader = t_bc
            rzbc_tok = em.op("vector", lambda e, pz=pz: e.tensor_copy(rzbc[:, :], pz[:, :]),
                             waits=[t_bc, rzbc_tok], inc="sDVE")
            bank_free[4] = rzbc_tok
            t_fin = em.op("vector", lambda e, po=po, h=h, ql=ql: e.tensor_mul(
                oT_sb[:, h * TLOC + ql: h * TLOC + ql + 512], po[:, :], rzbc[:, :]),
                waits=[rzbc_tok, t_omm], inc="sDVE")
            bank_free[ob] = t_fin
        g_pe[hb] = t_bc
        g_act[hb] = last_te
    em.barrier()
    ar2.free()
    if dbg == "s2":
        for h in range(NH):
            em.dma("scalar", lambda h=h: oT_dbg[h, :, :],
                   lambda h=h: oT_sb[:, h * TLOC:(h + 1) * TLOC])
        em.op("scalar", lambda e: e.nop(), waits=[("sSC", em.cnt["sSC"])], inc="sACT")
        return nc, em

    ar3 = nc.auto_arena()
    wo_sb = ar3.sbuf_tensor("wo_sb", [128, NFC * D], BF16)
    bo_s = ar3.sbuf_tensor("bo_s", [1, D], F32)
    xres_t = [ar3.sbuf_tensor(f"xres{i}", [128, D], F32) for i in range(2)]
    x1pre = ar3.sbuf_tensor("x1pre", [128, D], F32)
    mean3 = ar3.sbuf_tensor("mean3", [128, 1], F32)
    var3 = ar3.sbuf_tensor("var3", [128, 1], F32)
    rstd3 = ar3.sbuf_tensor("rstd3", [128, 1], F32)
    xhi = ar3.sbuf_tensor("xhi", [128, D], BF16)
    xnf = ar3.sbuf_tensor("xnf", [128, D], F32)
    xlo = ar3.sbuf_tensor("xlo", [128, D], BF16)
    sthi = ar3.sbuf_tensor("sthi", [128, NFC * 128], BF16)
    stlo = ar3.sbuf_tensor("stlo", [128, NFC * 128], BF16)
    ar3.allocate()

    t_wo = []
    for c in range(NFC):
        t_wo.append(em.dma("gpsimd", lambda c=c: wo_sb[:, c * D:(c + 1) * D],
                           lambda c=c: wo_in[c * 128:(c + 1) * 128, :]))
    t_bo = em.dma("sync", lambda: bo_s[:, :], lambda: bo_in[:, :])
    t_xr = [None, None]
    tok_pre = None; tok_hi = None; tok_lo = None; tok_xn = None
    st_x1 = None; st_hi = None; st_lo = None
    for tt in range(16):
        xb = tt % 2
        t_xr[xb] = em.dma("sync", lambda xb=xb: xres_t[xb][:, :],
                          lambda tt=tt: xres_in[tt * 128:(tt + 1) * 128, :],
                          waits=[t_xr[xb], tok_pre])
        tmm = [None] * 4
        for dcol in range(4):
            ps = psf[dcol]
            em.op("tensor", lambda e, ps=ps, dcol=dcol: e.matmul(
                ps[:, :], ones512[0:1, 0:128], bo_s[0:1, dcol * 512:(dcol + 1) * 512],
                start=True, stop=False), waits=[bank_free[dcol], t_wo[-1], t_bo, t_ones])
        for c in range(NFC):
            for dcol in range(4):
                ps = psf[dcol]
                tmm[dcol] = em.op("tensor", lambda e, ps=ps, c=c, dcol=dcol, tt=tt: e.matmul(
                    ps[:, :], oT_sb[:, c * TLOC + tt * 128: c * TLOC + tt * 128 + 128],
                    wo_sb[:, c * D + dcol * 512: c * D + dcol * 512 + 512],
                    start=False, stop=(c == NFC - 1)),
                    inc="sMM" if c == NFC - 1 else None)
        for dcol in range(4):
            ps = psf[dcol]
            tok_pre = em.op("vector", lambda e, ps=ps, dcol=dcol, xb=xb: e.tensor_add(
                x1pre[:, dcol * 512:(dcol + 1) * 512], ps[:, :],
                xres_t[xb][:, dcol * 512:(dcol + 1) * 512]),
                waits=[tmm[dcol], t_xr[xb]], inc="sDVE")
            bank_free[dcol] = tok_pre
        t_mn = em.op("vector", lambda e: e.reduce_sum(mean3[:, :], x1pre[:, :], axis=AX.X),
                     waits=[tok_pre], inc="sDVE")
        t_mn = em.op("vector", lambda e: e.tensor_scalar_mul(mean3[:, :], mean3[:, :], 1.0 / D),
                     waits=[t_mn], inc="sDVE")
        t_xc = em.op("vector", lambda e: e.tensor_scalar(
            x1pre[:, :], x1pre[:, :], mean3[:, 0:1], None, ALU.subtract),
            waits=[t_mn], inc="sDVE")
        t_v = em.op("vector", lambda e: e.scalar_tensor_tensor(
            xnf[:, :], x1pre[:, :], 1.0, x1pre[:, :], ALU.mult, ALU.mult,
            accum_out=var3[:, 0:1]), waits=[t_xc, tok_xn, st_x1], inc="sDVE")
        t_r = em.op("scalar", lambda e: e.activation(
            rstd3[:, :], var3[:, :], AF.Ln, bias=eps_c[:, 0:1], scale=1.0 / D),
            waits=[t_v, t_eps], inc="sACT")
        t_r = em.op("scalar", lambda e: e.activation(
            rstd3[:, :], rstd3[:, :], AF.Exp, bias=0.0, scale=-0.5), waits=[t_r], inc="sACT")
        tok_xn = em.op("vector", lambda e: e.tensor_scalar(
            xnf[:, :], x1pre[:, :], rstd3[:, 0:1], None, ALU.mult),
            waits=[t_r], inc="sDVE")
        tok_hi = em.op("vector", lambda e: e.tensor_copy(xhi[:, :], xnf[:, :]),
                       waits=[tok_xn, tok_hi], inc="sDVE")
        tok_lo = em.op("vector", lambda e: e.tensor_sub(xlo[:, :], xnf[:, :], xhi[:, :]),
                       waits=[tok_hi, tok_lo], inc="sDVE")
        st_x1 = em.dma("scalar", lambda tt=tt: x1_d[tt * 128:(tt + 1) * 128, :],
                       lambda: xnf[:, :], waits=[tok_xn])
        tr_hi = None; tr_lo = None
        for c in range(NFC):
            t_t1 = em.op("tensor", lambda e, c=c: e.transpose(
                psb[:, 0:128], xhi[:, c * 128:(c + 1) * 128], idbf[:, :]),
                waits=[tok_hi, bankb_free[0]], inc="sMM")
            tr_hi = em.op("vector", lambda e, c=c: e.tensor_copy(
                sthi[:, c * 128:(c + 1) * 128], psb[:, 0:128]),
                waits=[t_t1, st_hi], inc="sDVE")
            bankb_free[0] = tr_hi
            t_t2 = em.op("tensor", lambda e, c=c: e.transpose(
                psb[:, 0:128], xlo[:, c * 128:(c + 1) * 128], idbf[:, :]),
                waits=[tok_lo, bankb_free[0]], inc="sMM")
            tr_lo = em.op("vector", lambda e, c=c: e.tensor_copy(
                stlo[:, c * 128:(c + 1) * 128], psb[:, 0:128]),
                waits=[t_t2, st_lo], inc="sDVE")
            bankb_free[0] = tr_lo
        for part in range(2):
            st_hi = em.dma("scalar", lambda tt=tt, part=part:
                           x1h_d[part * 8:(part + 1) * 8, :, tt * 128:(tt + 1) * 128]
                           .rearrange("c p t -> p c t"),
                           lambda part=part: sthi[:, part * 8 * 128:(part + 1) * 8 * 128]
                           .rearrange("p (c t) -> p c t", c=8),
                           waits=[tr_hi])
            st_lo = em.dma("scalar", lambda tt=tt, part=part:
                           x1l_d[part * 8:(part + 1) * 8, :, tt * 128:(tt + 1) * 128]
                           .rearrange("c p t -> p c t"),
                           lambda part=part: stlo[:, part * 8 * 128:(part + 1) * 8 * 128]
                           .rearrange("p (c t) -> p c t", c=8),
                           waits=[tr_lo])
    em.barrier()
    ar3.free()
    ar2o.free()
    if dbg == "s3":
        em.op("scalar", lambda e: e.nop(), waits=[("sSC", em.cnt["sSC"])], inc="sACT")
        return nc, em

    ar4 = nc.auto_arena()
    x1h_sb = ar4.sbuf_tensor("x1h_sb", [128, NFC * 1024], BF16)
    x1l_tt = [ar4.sbuf_tensor(f"x1l_tt{i}", [128, NFC * 128], BF16) for i in range(2)]
    rg_f = ar4.sbuf_tensor("rg_f", [128, NFC * NE], F32)
    rg_hi = ar4.sbuf_tensor("rg_hi", [128, NFC * NE], BF16)
    rg_lo = ar4.sbuf_tensor("rg_lo", [128, NFC * NE], BF16)
    wexp = ar4.sbuf_tensor("wexp", [128, 8 * NE], F32)
    leq1 = ar4.sbuf_tensor("leq1", [128, NE], F32)
    leq2 = ar4.sbuf_tensor("leq2", [128, NE], F32)
    lm1 = ar4.sbuf_tensor("lm1", [128, 1], F32)
    lm2 = ar4.sbuf_tensor("lm2", [128, 1], F32)
    lw1 = ar4.sbuf_tensor("lw1", [128, 1], F32)
    lw2 = ar4.sbuf_tensor("lw2", [128, 1], F32)
    lt1 = ar4.sbuf_tensor("lt1", [128, NE], F32)
    acc = [ar4.sbuf_tensor(f"acc{i}", [128, D], F32) for i in range(8)]
    h_sb = ar4.sbuf_tensor("h_sb", [128, 8 * 1024], BF16)
    wg_sb = [ar4.sbuf_tensor(f"wg_sb{i}", [128, NFC * 128], BF16) for i in range(2)]
    wu_sb = [ar4.sbuf_tensor(f"wu_sb{i}", [128, NFC * 128], BF16) for i in range(2)]
    wd_sb = [ar4.sbuf_tensor(f"wd_sb{i}", [128, 8 * 512], BF16) for i in range(2)]
    sg_sb = [ar4.sbuf_tensor(f"sg_sb{i}", [128, 512], F32) for i in range(2)]
    stout = [ar4.sbuf_tensor(f"stout{i}", [128, D], F32) for i in range(2)]
    mean4 = ar4.sbuf_tensor("mean4", [128, 1], F32)
    var4 = ar4.sbuf_tensor("var4", [128, 1], F32)
    rstd4 = ar4.sbuf_tensor("rstd4", [128, 1], F32)
    tmp4 = ar4.sbuf_tensor("tmp4", [128, D], F32)
    ar4.allocate()

    t_rg = []
    for part in range(4):
        t_rg.append(em.dma("sync",
                           lambda part=part: rg_f[:, part * 4 * NE:(part + 1) * 4 * NE]
                           .rearrange("p (c e) -> p c e", c=4),
                           lambda part=part: rgt_in[part * 512:(part + 1) * 512, :]
                           .rearrange("(c p) e -> p c e", p=128)))
    t_rgh = em.op("vector", lambda e: e.tensor_copy(rg_hi[:, :], rg_f[:, :]),
                  waits=[t_rg[-1]], inc="sDVE")
    t_rgl = em.op("vector", lambda e: e.tensor_sub(rg_lo[:, :], rg_f[:, :], rg_hi[:, :]),
                  waits=[t_rgh], inc="sDVE")

    t_x1h = None
    t_x1l = [None, None]
    x1l_guard = [None, None]
    t_acc = [None] * 8
    t_wgl = [None, None]; t_wul = [None, None]; t_wdl = [None, None]
    wg_guard = [None, None]; wu_guard = [None, None]; wd_guard = [None, None]
    sg_tok = [None, None]
    h_tok = None
    h_guard = None
    x1h_guard = None
    acc_tok = [None] * 8
    st_out = [None, None]
    wx_tok = None
    last_store = None

    for half in range(2):
        hoff = half * 1024
        for part in range(2):
            t_x1h = em.dma("sync", lambda part=part: x1h_sb[:, part * 8 * 1024:(part + 1) * 8 * 1024]
                           .rearrange("p (c t) -> p c t", c=8),
                           lambda hoff=hoff, part=part: x1h_d[part * 8:(part + 1) * 8, :, hoff:hoff + 1024]
                           .rearrange("c p t -> p c t"), waits=[t_x1h, x1h_guard])
        for tt in range(8):
            t_acc[tt] = em.dma("sync", lambda tt=tt: acc[tt][:, :],
                               lambda tt=tt, hoff=hoff: x1_d[hoff + tt * 128: hoff + (tt + 1) * 128, :],
                               waits=[acc_tok[tt], t_acc[tt]])
        for tt in range(8):
            xb = tt % 2
            t_x1l[xb] = em.dma("sync", lambda xb=xb: x1l_tt[xb][:, :]
                               .rearrange("p (c t) -> p c t", c=NFC),
                               lambda tt=tt, hoff=hoff: x1l_d[:, :, hoff + tt * 128: hoff + (tt + 1) * 128]
                               .rearrange("c p t -> p c t"), waits=[t_x1l[xb], x1l_guard[xb]])
            pr = psf[5]
            t_l = None
            for f in range(NFC):
                em.op("tensor", lambda e, pr=pr, f=f, tt=tt: e.matmul(
                    pr[:, 0:NE], x1h_sb[:, f * 1024 + tt * 128: f * 1024 + tt * 128 + 128],
                    rg_hi[:, f * NE:(f + 1) * NE], start=(f == 0), stop=False),
                    waits=[t_x1h, t_rgh] + ([bank_free[5]] if f == 0 else []))
                em.op("tensor", lambda e, pr=pr, f=f, tt=tt: e.matmul(
                    pr[:, 0:NE], x1h_sb[:, f * 1024 + tt * 128: f * 1024 + tt * 128 + 128],
                    rg_lo[:, f * NE:(f + 1) * NE], start=False, stop=False),
                    waits=[t_rgl] if f == 0 else [])
                t_l = em.op("tensor", lambda e, pr=pr, f=f, xb=xb: e.matmul(
                    pr[:, 0:NE], x1l_tt[xb][:, f * 128:(f + 1) * 128],
                    rg_hi[:, f * NE:(f + 1) * NE], start=False, stop=(f == NFC - 1)),
                    waits=[t_x1l[xb]] if f == 0 else [], inc="sMM" if f == NFC - 1 else None)
            x1l_guard[xb] = t_l
            t1 = em.op("vector", lambda e, pr=pr: e.reduce_max(lm1[:, :], pr[:, 0:NE], axis=AX.X),
                       waits=[t_l, wx_tok], inc="sDVE")
            t2 = em.op("vector", lambda e, pr=pr: e.tensor_scalar(
                leq1[:, :], pr[:, 0:NE], lm1[:, 0:1], None, ALU.is_equal), waits=[t1], inc="sDVE")
            t3 = em.op("vector", lambda e: e.tensor_scalar_mul(lt1[:, :], leq1[:, :], -1e30),
                       waits=[t2], inc="sDVE")
            t4 = em.op("vector", lambda e, pr=pr: e.tensor_add(lt1[:, :], lt1[:, :], pr[:, 0:NE]),
                       waits=[t3], inc="sDVE")
            bank_free[5] = t4
            t5 = em.op("vector", lambda e: e.reduce_max(lm2[:, :], lt1[:, :], axis=AX.X),
                       waits=[t4], inc="sDVE")
            t6 = em.op("vector", lambda e: e.tensor_scalar(
                leq2[:, :], lt1[:, :], lm2[:, 0:1], None, ALU.is_equal), waits=[t5], inc="sDVE")
            t7 = em.op("vector", lambda e: e.tensor_scalar(
                lm2[:, :], lm2[:, :], lm1[:, 0:1], None, ALU.subtract), waits=[t6], inc="sDVE")
            t8 = em.op("scalar", lambda e: e.activation(
                lw1[:, :], lm2[:, :], AF.Exp, bias=0.0, scale=1.0), waits=[t7], inc="sACT")
            t9 = em.op("vector", lambda e: e.tensor_scalar_add(lw2[:, :], lw1[:, :], 1.0),
                       waits=[t8], inc="sDVE")
            t10 = em.op("vector", lambda e: e.reciprocal(lw2[:, :], lw2[:, :]),
                        waits=[t9], inc="sDVE")
            t11 = em.op("vector", lambda e: e.tensor_mul(lw1[:, :], lw1[:, :], lw2[:, :]),
                        waits=[t10], inc="sDVE")
            t12 = em.op("vector", lambda e: e.tensor_scalar(
                lt1[:, :], leq1[:, :], lw2[:, 0:1], None, ALU.mult), waits=[t11], inc="sDVE")
            wx_tok = em.op("vector", lambda e, tt=tt: e.scalar_tensor_tensor(
                wexp[:, tt * NE:(tt + 1) * NE], leq2[:, :], lw1[:, 0:1], lt1[:, :],
                ALU.mult, ALU.add), waits=[t12], inc="sDVE")
        for e_i in range(NE):
            for cc in range(8):
                wb = (e_i * 8 + cc) % 2
                t_wgl[wb] = em.dma("gpsimd", lambda wb=wb: wg_sb[wb][:, :]
                                   .rearrange("p (c d) -> p c d", c=NFC),
                                   lambda e_i=e_i, cc=cc: wg_in[e_i, :, cc * 128:(cc + 1) * 128]
                                   .rearrange("(c p) d -> p c d", p=128),
                                   waits=[t_wgl[wb], wg_guard[wb]])
                t_wul[wb] = em.dma("gpsimd", lambda wb=wb: wu_sb[wb][:, :]
                                   .rearrange("p (c d) -> p c d", c=NFC),
                                   lambda e_i=e_i, cc=cc: wu_in[e_i, :, cc * 128:(cc + 1) * 128]
                                   .rearrange("(c p) d -> p c d", p=128),
                                   waits=[t_wul[wb], wu_guard[wb]])
                for tg in range(2):
                    pg = psf[tg]; pu = psf[2 + tg]
                    t_g = None; t_u = None
                    for f in range(NFC):
                        t_g = em.op("tensor", lambda e, pg=pg, f=f, tg=tg, wb=wb: e.matmul(
                            pg[:, :], wg_sb[wb][:, f * 128:(f + 1) * 128],
                            x1h_sb[:, f * 1024 + tg * 512: f * 1024 + tg * 512 + 512],
                            start=(f == 0), stop=(f == NFC - 1)),
                            waits=[t_wgl[wb], t_x1h, bank_free[tg]] if f == 0 else [],
                            inc="sMM" if f == NFC - 1 else None)
                    for f in range(NFC):
                        t_u = em.op("tensor", lambda e, pu=pu, f=f, tg=tg, wb=wb: e.matmul(
                            pu[:, :], wu_sb[wb][:, f * 128:(f + 1) * 128],
                            x1h_sb[:, f * 1024 + tg * 512: f * 1024 + tg * 512 + 512],
                            start=(f == 0), stop=(f == NFC - 1)),
                            waits=[t_wul[wb], bank_free[2 + tg]] if f == 0 else [],
                            inc="sMM" if f == NFC - 1 else None)
                    wg_guard[wb] = t_g
                    wu_guard[wb] = t_u
                    x1h_guard = t_u
                    t_sg = em.op("scalar", lambda e, pg=pg, tg=tg: e.activation(
                        sg_sb[tg][:, :], pg[:, :], AF.Silu, bias=0.0, scale=1.0),
                        waits=[t_g, sg_tok[tg]], inc="sACT")
                    bank_free[tg] = t_sg
                    h_tok = em.op("vector", lambda e, pu=pu, tg=tg, cc=cc: e.tensor_mul(
                        h_sb[:, cc * 1024 + tg * 512: cc * 1024 + tg * 512 + 512],
                        pu[:, :], sg_sb[tg][:, :]),
                        waits=[t_u, t_sg, h_tok, h_guard], inc="sDVE")
                    sg_tok[tg] = h_tok
                    bank_free[2 + tg] = h_tok
            for mcol in range(4):
                wb = (e_i * 4 + mcol) % 2
                t_wdl[wb] = em.dma("gpsimd", lambda wb=wb: wd_sb[wb][:, :]
                                   .rearrange("p (c m) -> p c m", c=8),
                                   lambda e_i=e_i, mcol=mcol: wd_in[e_i, :, mcol * 512:(mcol + 1) * 512]
                                   .rearrange("(c p) m -> p c m", p=128),
                                   waits=[t_wdl[wb], wd_guard[wb]])
                for tt in range(8):
                    pd = psf[4 + (tt % 2)]
                    t_d = None
                    for cc in range(8):
                        t_d = em.op("tensor", lambda e, pd=pd, cc=cc, tt=tt, wb=wb: e.matmul(
                            pd[:, :], h_sb[:, cc * 1024 + tt * 128: cc * 1024 + tt * 128 + 128],
                            wd_sb[wb][:, cc * 512:(cc + 1) * 512],
                            start=(cc == 0), stop=(cc == 7)),
                            waits=[t_wdl[wb], h_tok, bank_free[4 + (tt % 2)]] if cc == 0 else [],
                            inc="sMM" if cc == 7 else None)
                    wd_guard[wb] = t_d
                    h_guard = t_d
                    acc_tok[tt] = em.op("vector", lambda e, pd=pd, tt=tt, mcol=mcol, e_i=e_i:
                                        e.scalar_tensor_tensor(
                                            acc[tt][:, mcol * 512:(mcol + 1) * 512], pd[:, :],
                                            wexp[:, tt * NE + e_i: tt * NE + e_i + 1],
                                            acc[tt][:, mcol * 512:(mcol + 1) * 512],
                                            ALU.mult, ALU.add),
                                        waits=[t_d, wx_tok, t_acc[tt], acc_tok[tt]], inc="sDVE")
                    bank_free[4 + (tt % 2)] = acc_tok[tt]
        for tt in range(8):
            ob = tt % 2
            t_mn = em.op("vector", lambda e, tt=tt: e.reduce_sum(
                mean4[:, :], acc[tt][:, :], axis=AX.X), waits=[acc_tok[tt]], inc="sDVE")
            t_mn = em.op("vector", lambda e: e.tensor_scalar_mul(
                mean4[:, :], mean4[:, :], 1.0 / D), waits=[t_mn], inc="sDVE")
            t_xc = em.op("vector", lambda e, tt=tt: e.tensor_scalar(
                acc[tt][:, :], acc[tt][:, :], mean4[:, 0:1], None, ALU.subtract),
                waits=[t_mn], inc="sDVE")
            t_v = em.op("vector", lambda e, tt=tt: e.scalar_tensor_tensor(
                tmp4[:, :], acc[tt][:, :], 1.0, acc[tt][:, :], ALU.mult, ALU.mult,
                accum_out=var4[:, 0:1]), waits=[t_xc], inc="sDVE")
            t_r = em.op("scalar", lambda e: e.activation(
                rstd4[:, :], var4[:, :], AF.Ln, bias=eps_c[:, 0:1], scale=1.0 / D),
                waits=[t_v, t_eps], inc="sACT")
            t_r = em.op("scalar", lambda e: e.activation(
                rstd4[:, :], rstd4[:, :], AF.Exp, bias=0.0, scale=-0.5), waits=[t_r], inc="sACT")
            t_o = em.op("vector", lambda e, tt=tt, ob=ob: e.tensor_scalar(
                stout[ob][:, :], acc[tt][:, :], rstd4[:, 0:1], None, ALU.mult),
                waits=[t_r, st_out[ob]], inc="sDVE")
            acc_tok[tt] = t_o
            st_out[ob] = em.dma("scalar", lambda tt=tt, hoff=hoff:
                                out_d[hoff + tt * 128: hoff + (tt + 1) * 128, :],
                                lambda ob=ob: stout[ob][:, :], waits=[t_o])
            last_store = st_out[ob]
    em.op("scalar", lambda e: e.nop(), waits=[last_store], inc="sACT")

    return nc, em


def _host_inputs(full):
    x = np.asarray(full["L0_attn_x"], dtype=np.float32)
    wq = np.asarray(full["L0_attn_Wq"], dtype=np.float32)
    wk = np.asarray(full["L0_attn_Wk"], dtype=np.float32)
    wv = np.asarray(full["L0_attn_Wv"], dtype=np.float32)
    wo = np.asarray(full["L0_attn_Wo"], dtype=np.float32)
    bq = np.asarray(full["L0_attn_bq"], dtype=np.float32)
    bk = np.asarray(full["L0_attn_bk"], dtype=np.float32)
    bv = np.asarray(full["L0_attn_bv"], dtype=np.float32)
    bo = np.asarray(full["L0_attn_bo"], dtype=np.float32)
    rg = np.asarray(full["L0_ffn_router_gate"], dtype=np.float32)
    wgu = np.asarray(full["L0_ffn_expert_gate_up"], dtype=np.float32)
    wd = np.asarray(full["L0_ffn_expert_down"], dtype=np.float32)

    perm = np.zeros(D, dtype=np.int64)
    for hh in range(NH):
        base = hh * HS
        perm[base:base + 64] = base + 2 * np.arange(64)
        perm[base + 64:base + 128] = base + 2 * np.arange(64) + 1
    wq_p = np.ascontiguousarray(wq[:, perm])
    wk_p = np.ascontiguousarray(wk[:, perm])
    bq_p = np.ascontiguousarray(bq[perm])[None, :]
    bk_p = np.ascontiguousarray(bk[perm])[None, :]

    inv_freq = 1.0 / (10000.0 ** (np.arange(0, HS, 2, dtype=np.float64) / HS))
    tpos = np.arange(T, dtype=np.float64)
    ang = tpos[None, :] * inv_freq[:, None]
    cosv = np.cos(ang).astype(np.float32)
    sinv = np.sin(ang).astype(np.float32)
    ckf = np.ascontiguousarray(np.concatenate([cosv, cosv], axis=0))
    skf = np.ascontiguousarray(np.concatenate([-sinv, sinv], axis=0))

    swap = np.ascontiguousarray(np.roll(np.eye(128, dtype=np.float32), 64, axis=0))
    id32 = np.eye(128, dtype=np.float32)
    rgt = np.ascontiguousarray(rg.T)
    wg = np.ascontiguousarray(wgu[:, :, 0::2])
    wu = np.ascontiguousarray(wgu[:, :, 1::2])

    ins = []
    for c in range(8):
        b, hf = c // 2, c % 2
        gl = G_OF_HALF[hf]
        cols = np.concatenate([np.arange(512 * g, 512 * g + 512) for g in gl])
        xt = np.ascontiguousarray(x[b].T)
        xtq = np.ascontiguousarray(xt[:, cols])
        xres = np.ascontiguousarray(x[b][cols, :])
        cqf = np.ascontiguousarray(ckf[:, cols])
        sqf = np.ascontiguousarray(skf[:, cols])
        masks = np.zeros((4, 8, 128, 512), dtype=np.float32)
        for l in range(4):
            g = gl[l]
            cnt = KBCNT[l]
            for s in range(8):
                kb = cnt - 8 + s
                jj = kb * 128 + np.arange(128)[:, None]
                qq = 512 * g + np.arange(512)[None, :]
                masks[l, s] = (jj <= qq).astype(np.float32)
        ins.append({
            "xt": xt, "xtq": xtq, "xres": xres,
            "wq": wq_p, "wk": wk_p, "wv": wv, "wo": wo,
            "bq": bq_p, "bk": bk_p, "bv": np.ascontiguousarray(bv[None, :]),
            "bo": np.ascontiguousarray(bo[None, :]),
            "ckf": ckf, "skf": skf, "cqf": cqf, "sqf": sqf,
            "swp": swap, "id32": id32,
            "masks": masks.astype(ml_dtypes.bfloat16),
            "rgt": rgt, "wg": wg, "wu": wu, "wd": wd,
        })
    return ins


def kernel(**inputs):
    ins = _host_inputs(inputs)
    nc, em = build()
    stk = materialize(nc, em)
    try:
        res = run_bass_kernel_spmd(nc, ins, core_ids=list(range(8)))
    finally:
        stk.close()
    out = np.zeros((B, T, D), dtype=np.float32)
    for c in range(8):
        b, hf = c // 2, c % 2
        gl = G_OF_HALF[hf]
        cols = np.concatenate([np.arange(512 * g, 512 * g + 512) for g in gl])
        out[b][cols, :] = res.results[c]["out"]
    return out


# version 04
# speedup vs baseline: 1.0316x; 1.0316x over previous
import math
import numpy as np
import ml_dtypes
import concourse.bass as bass
import concourse.mybir as mybir
from concourse.bass_utils import run_bass_kernel_spmd

F32 = mybir.dt.float32
BF16 = mybir.dt.bfloat16
AF = mybir.ActivationFunctionType
ALU = mybir.AluOpType
AX = mybir.AxisListType

B, T, D, NH, HS, NE, DI = 4, 4096, 2048, 16, 128, 8, 1024
TLOC = 2048
NFC = D // 128
SCALE = 1.0 / math.sqrt(HS)
AMGM = 0.5 * SCALE
EPS = 1e-5
KBCNT = (8, 16, 24, 32)
G_OF_HALF = ((0, 3, 4, 7), (1, 2, 5, 6))

ENGS = ("sync", "gpsimd", "tensor", "vector", "scalar")
SEMS = ("sSY", "sGP", "sSC", "sMM", "sDVE", "sACT")


def _flat(waits):
    out = []
    for w in waits:
        if w is None:
            continue
        if isinstance(w, list):
            out.extend(x for x in w if x is not None)
        else:
            out.append(w)
    return tuple(out)


class Emit:
    def __init__(self):
        self.ops = {e: [] for e in ENGS}
        self.cnt = {s: 0 for s in SEMS}
        self.last_inc = {e: True for e in ENGS}

    def op(self, eng, fn, waits=(), inc=None, amt=1):
        tok = None
        if inc is not None:
            self.cnt[inc] += amt
            tok = (inc, self.cnt[inc])
        self.ops[eng].append((fn, _flat(waits), (inc, amt) if inc else None))
        self.last_inc[eng] = inc is not None
        return tok

    def dma(self, queue, out_fn, in_fn, waits=()):
        sem = {"sync": "sSY", "gpsimd": "sGP", "scalar": "sSC"}[queue]
        def fn(e, o=out_fn, i=in_fn):
            return e.dma_start(o(), i())
        return self.op(queue, fn, waits=waits, inc=sem, amt=16)

    def barrier(self):
        if not self.last_inc["tensor"]:
            self.op("tensor", lambda e: e.nop(), inc="sMM")
        if not self.last_inc["vector"]:
            self.op("vector", self.dummy_vec, inc="sDVE")
        if not self.last_inc["scalar"]:
            self.op("scalar", self.dummy_act, inc="sACT")
        snap = tuple((s, v) for s, v in self.cnt.items() if v > 0)
        for e in ENGS:
            self.ops[e].append((None, snap, None))


def materialize(nc, em):
    import contextlib
    stk = contextlib.ExitStack()
    sems = {s: stk.enter_context(nc.semaphore(s)) for s in SEMS}
    with nc.Block() as block:
        for eng in ENGS:
            ops = em.ops[eng]

            def mk(ops=ops):
                def f(e):
                    hw = {s: 0 for s in SEMS}
                    for fn, waits, inc in ops:
                        for (s, v) in waits:
                            if v > hw[s]:
                                e.wait_ge(sems[s], v)
                                hw[s] = v
                        if fn is None:
                            continue
                        inst = fn(e)
                        if inc:
                            inst.then_inc(sems[inc[0]], inc[1])
                return f
            getattr(block, eng)(mk())
    return stk


DBG_STAGE = None
S_BANKS_CFG = (0, 1, 5, 6)
DL_CFG = 4


def build():
    dbg = DBG_STAGE
    nc = bass.Bass("TRN2", target_bir_lowering=False)
    em = Emit()

    din = {}
    def I(name, shape, dt=F32):
        din[name] = nc.dram_tensor(name, list(shape), dt, kind="ExternalInput")
        return din[name]

    xt_in = I("xt", (D, T))
    xtq_in = I("xtq", (D, TLOC))
    xres_in = I("xres", (TLOC, D))
    wq_in = I("wq", (D, D)); wk_in = I("wk", (D, D))
    wv_in = I("wv", (D, D)); wo_in = I("wo", (D, D))
    bq_in = I("bq", (1, D)); bk_in = I("bk", (1, D))
    bv_in = I("bv", (1, D)); bo_in = I("bo", (1, D))
    ckf_in = I("ckf", (128, T)); skf_in = I("skf", (128, T))
    cqf_in = I("cqf", (128, TLOC)); sqf_in = I("sqf", (128, TLOC))
    swap_in = I("swp", (128, 128))
    id32_in = I("id32", (128, 128))
    masks_in = I("masks", (4, 8, 128, 512), BF16)
    rgt_in = I("rgt", (D, NE))
    wg_in = I("wg", (NE, D, DI)); wu_in = I("wu", (NE, D, DI))
    wd_in = I("wd", (NE, DI, D))

    out_d = nc.dram_tensor("out", [TLOC, D], F32, kind="ExternalOutput")

    k1 = "ExternalOutput" if dbg == "s1" else "Internal"
    k3 = "ExternalOutput" if dbg == "s3" else "Internal"
    qT_d = nc.dram_tensor("qT_d", [NH, 128, TLOC], BF16, kind=k1)
    kT_d = nc.dram_tensor("kT_d", [NH, 128, T], BF16, kind=k1)
    vT_d = nc.dram_tensor("vT_d", [NH, 128, T], BF16, kind=k1)
    x1_d = nc.dram_tensor("x1_d", [TLOC, D], F32, kind=k3)
    x1h_d = nc.dram_tensor("x1h_d", [NFC, 128, TLOC], BF16, kind=k3)
    x1l_d = nc.dram_tensor("x1l_d", [NFC, 128, TLOC], BF16, kind=k3)
    if dbg == "s2":
        oT_dbg = nc.dram_tensor("oT_dbg", [NH, 128, TLOC], BF16, kind="ExternalOutput")

    import contextlib
    pstk = contextlib.ExitStack()
    psf = [pstk.enter_context(nc.psum_tensor(f"psf{i}", [128, 512], F32)) for i in range(7)]
    psb = pstk.enter_context(nc.psum_tensor("psb", [128, 1024], BF16))
    bank_free = {i: None for i in range(7)}
    bankb_free = [None]

    ar0 = nc.auto_arena()
    ones512 = ar0.sbuf_tensor("ones512", [1, 512], F32)
    ones1b = ar0.sbuf_tensor("ones1b", [128, 1], BF16)
    idbf = ar0.sbuf_tensor("idbfs", [128, 128], BF16)
    swp = ar0.sbuf_tensor("swps", [128, 128], F32)
    dum0 = ar0.sbuf_tensor("dum0", [1, 16], F32)
    eps_c = ar0.sbuf_tensor("eps_c", [128, 1], F32)
    ar0.allocate()
    em.dummy_vec = lambda e: e.memset(dum0[:, :], 0.0)
    em.dummy_act = lambda e: e.copy(dum0[0:1, 0:16], dum0[0:1, 0:16])
    t_eps = em.op("vector", lambda e: e.memset(eps_c[:, :], EPS), inc="sDVE")

    t_ones = em.op("vector", lambda e: e.memset(ones512[:, :], 1.0), inc="sDVE")
    t_ones1b = em.op("vector", lambda e: e.memset(ones1b[:, :], 1.0), inc="sDVE")
    t_idb = em.dma("gpsimd", lambda: idbf[:, :], lambda: id32_in[:, :])
    t_swp = em.dma("sync", lambda: swp[:, :], lambda: swap_in[:, :])

    ar1 = nc.auto_arena()
    xt_sb = ar1.sbuf_tensor("xt_sb", [128, NFC * T], BF16)
    csk = ar1.sbuf_tensor("csk", [128, T], BF16)
    snk = ar1.sbuf_tensor("snk", [128, T], BF16)
    bk_s = ar1.sbuf_tensor("bk_s", [1, D], F32)
    bv_s = ar1.sbuf_tensor("bv_s", [1, D], F32)
    w1_sb = [ar1.sbuf_tensor(f"w1_{i}", [128, NFC * 128], BF16) for i in range(2)]
    qsb1 = [ar1.sbuf_tensor(f"qsb1_{i}", [128, 512], F32) for i in range(2)]
    ta1 = [ar1.sbuf_tensor(f"ta1_{i}", [128, 512], F32) for i in range(2)]
    tb1 = [ar1.sbuf_tensor(f"tb1_{i}", [128, 512], F32) for i in range(2)]
    stg1 = [ar1.sbuf_tensor(f"stg1_{i}", [128, 512], BF16) for i in range(4)]
    ar1.allocate()

    t_xt = []
    for c in range(NFC):
        t_xt.append(em.dma("gpsimd", lambda c=c: xt_sb[:, c * T:(c + 1) * T],
                           lambda c=c: xt_in[c * 128:(c + 1) * 128, :]))
    t_cs = em.dma("gpsimd", lambda: csk[:, :], lambda: ckf_in[:, :])
    t_sn = em.dma("gpsimd", lambda: snk[:, :], lambda: skf_in[:, :])
    t_bk = em.dma("sync", lambda: bk_s[:, :], lambda: bk_in[:, :])
    t_bv = em.dma("sync", lambda: bv_s[:, :], lambda: bv_in[:, :])

    def w_colchunk_dma(w_in, dc, dst, waits):
        def src(w_in=w_in, dc=dc):
            return w_in[:, dc * 128:(dc + 1) * 128].rearrange("(c p) d -> p c d", p=128)
        def dstf(dst=dst):
            return dst[:, :].rearrange("p (c d) -> p c d", c=NFC)
        return em.dma("gpsimd", dstf, src, waits=waits)

    def proj_rope(w_in, b_s, t_b, x_sb, x_tok, t_x_last, cs_sb, sn_sb, t_cst, t_snt,
                  dst_d, n_tg, w_bufs, qsb, ta, tb, stg, pst):
        w_tok = pst["w"]
        w_guard = pst["wg"]
        qs_tok = [None, None]
        ta_tok = [None, None]
        tb_tok = [None, None]
        st_tok = pst["st"]
        pending = [None]
        for dc in range(NH):
            wb = dc % 2
            w_tok[wb] = w_colchunk_dma(w_in, dc, w_bufs[wb], waits=[w_tok[wb], w_guard[wb]])
            for tg in range(n_tg):
                i = (dc * n_tg + tg) % 2
                ps = psf[i]; psw = psf[2 + i]
                em.op("tensor", lambda e, ps=ps, dc=dc, b_s=b_s: e.matmul(
                    ps[:, :], b_s[0:1, dc * 128:(dc + 1) * 128], ones512[0:1, :],
                    start=True, stop=False),
                    waits=[bank_free[i], w_tok[wb], t_ones, t_b, t_x_last])
                tk = None
                for f in range(NFC):
                    last = f == NFC - 1
                    tk = em.op("tensor", lambda e, ps=ps, f=f, tg=tg, wb=wb, x_sb=x_sb,
                               x_tok=x_tok, w_bufs=w_bufs: e.matmul(
                        ps[:, :], w_bufs[wb][:, f * 128:(f + 1) * 128],
                        x_sb[:, f * x_tok + tg * 512: f * x_tok + tg * 512 + 512],
                        start=False, stop=last), inc="sMM" if last else None)
                w_guard[wb] = tk
                qs_tok[i] = em.op("vector", lambda e, qs=qsb[i], ps=ps: e.tensor_copy(
                    qs[:, :], ps[:, :]), waits=[tk, qs_tok[i]], inc="sDVE")
                bank_free[i] = qs_tok[i]

                def mk_pending(i=i, dc=dc, tg=tg, psw=psw, qs_t=qs_tok[i]):
                    def go():
                        tks = em.op("tensor", lambda e, psw=psw, qs=qsb[i]: e.matmul(
                            psw[:, :], swp[:, :], qs[:, :], start=True, stop=True),
                            waits=[qs_t, bank_free[2 + i], t_swp], inc="sMM")
                        col = tg * 512
                        tka = em.op("vector", lambda e, t_=ta[i], qs=qsb[i], col=col, cs=cs_sb:
                                    e.tensor_mul(t_[:, :], qs[:, :], cs[:, col:col + 512]),
                                    waits=[qs_t, t_cst, ta_tok[i]], inc="sDVE")
                        tkb = em.op("vector", lambda e, t_=tb[i], psw=psw, col=col, sn=sn_sb:
                                    e.tensor_mul(t_[:, :], psw[:, :], sn[:, col:col + 512]),
                                    waits=[tks, t_snt, tb_tok[i]], inc="sDVE")
                        bank_free[2 + i] = tkb
                        si = (dc * n_tg + tg) % 4
                        tke = em.op("vector", lambda e, st=stg[si], a=ta[i], b_=tb[i]:
                                    e.tensor_add(st[:, :], a[:, :], b_[:, :]),
                                    waits=[tka, tkb, st_tok[si]], inc="sDVE")
                        ta_tok[i] = tke
                        tb_tok[i] = tke
                        st_tok[si] = em.dma("scalar",
                                            lambda dc=dc, tg=tg, dst_d=dst_d:
                                            dst_d[dc, :, tg * 512:(tg + 1) * 512],
                                            lambda si=si, stg=stg: stg[si][:, :], waits=[tke])
                    return go
                if pending[0] is not None:
                    pending[0]()
                pending[0] = mk_pending()
        pending[0]()
        pending[0] = None
        return st_tok

    def proj_plain(w_in, b_s, t_b, x_sb, x_tok, t_x_last, dst_d, n_tg, w_bufs, stg, pst):
        w_tok = pst["w"]
        w_guard = pst["wg"]
        st_tok = pst["st"]
        for dc in range(NH):
            wb = dc % 2
            w_tok[wb] = w_colchunk_dma(w_in, dc, w_bufs[wb], waits=[w_tok[wb], w_guard[wb]])
            for tg in range(n_tg):
                i = (dc * n_tg + tg) % 2
                ps = psf[i]
                em.op("tensor", lambda e, ps=ps, dc=dc, b_s=b_s: e.matmul(
                    ps[:, :], b_s[0:1, dc * 128:(dc + 1) * 128], ones512[0:1, :],
                    start=True, stop=False),
                    waits=[bank_free[i], w_tok[wb], t_ones, t_b, t_x_last])
                tk = None
                for f in range(NFC):
                    last = f == NFC - 1
                    tk = em.op("tensor", lambda e, ps=ps, f=f, tg=tg, wb=wb: e.matmul(
                        ps[:, :], w_bufs[wb][:, f * 128:(f + 1) * 128],
                        x_sb[:, f * x_tok + tg * 512: f * x_tok + tg * 512 + 512],
                        start=False, stop=last), inc="sMM" if last else None)
                w_guard[wb] = tk
                si = (dc * n_tg + tg) % 4
                tke = em.op("vector", lambda e, st=stg[si], ps=ps: e.tensor_copy(
                    st[:, :], ps[:, :]), waits=[tk, st_tok[si]], inc="sDVE")
                bank_free[i] = tke
                st_tok[si] = em.dma("scalar",
                                    lambda dc=dc, tg=tg: vT_d[dc, :, tg * 512:(tg + 1) * 512],
                                    lambda si=si, stg=stg: stg[si][:, :], waits=[tke])

    pst1 = {"w": [None, None], "wg": [None, None], "st": [None] * 4}
    proj_rope(wk_in, bk_s, t_bk, xt_sb, T, t_xt[-1], csk, snk, t_cs, t_sn,
              kT_d, 8, w1_sb, qsb1, ta1, tb1, stg1, pst1)
    proj_plain(wv_in, bv_s, t_bv, xt_sb, T, t_xt[-1], vT_d, 8, w1_sb, stg1, pst1)
    em.barrier()
    ar1.free()

    ar1b = nc.auto_arena()
    xtq_sb = ar1b.sbuf_tensor("xtq_sb", [128, NFC * TLOC], BF16)
    csq = ar1b.sbuf_tensor("csq", [128, TLOC], BF16)
    snq = ar1b.sbuf_tensor("snq", [128, TLOC], BF16)
    bq_s = ar1b.sbuf_tensor("bq_s", [1, D], F32)
    w1b_sb = [ar1b.sbuf_tensor(f"w1b_{i}", [128, NFC * 128], BF16) for i in range(2)]
    qsb1b = [ar1b.sbuf_tensor(f"qsb1b_{i}", [128, 512], F32) for i in range(2)]
    ta1b = [ar1b.sbuf_tensor(f"ta1b_{i}", [128, 512], F32) for i in range(2)]
    tb1b = [ar1b.sbuf_tensor(f"tb1b_{i}", [128, 512], F32) for i in range(2)]
    stg1b = [ar1b.sbuf_tensor(f"stg1b_{i}", [128, 512], BF16) for i in range(4)]
    ar1b.allocate()

    t_xtq = []
    for c in range(NFC):
        t_xtq.append(em.dma("gpsimd", lambda c=c: xtq_sb[:, c * TLOC:(c + 1) * TLOC],
                            lambda c=c: xtq_in[c * 128:(c + 1) * 128, :]))
    t_csq = em.dma("gpsimd", lambda: csq[:, :], lambda: cqf_in[:, :])
    t_snq = em.dma("gpsimd", lambda: snq[:, :], lambda: sqf_in[:, :])
    t_bq = em.dma("sync", lambda: bq_s[:, :], lambda: bq_in[:, :])
    pst1b = {"w": [None, None], "wg": [None, None], "st": [None] * 4}
    proj_rope(wq_in, bq_s, t_bq, xtq_sb, TLOC, t_xtq[-1], csq, snq, t_csq, t_snq,
              qT_d, 4, w1b_sb, qsb1b, ta1b, tb1b, stg1b, pst1b)
    em.barrier()
    ar1b.free()
    if dbg == "s1":
        em.op("scalar", lambda e: e.nop(), waits=[("sSC", em.cnt["sSC"])], inc="sACT")
        return nc, em

    ar2o = nc.auto_arena()
    oT_sb = ar2o.sbuf_tensor("oT_sb", [128, NH * TLOC], BF16)
    ar2o.allocate()
    ar2 = nc.auto_arena()
    q_h = [ar2.sbuf_tensor(f"q_h{i}", [128, TLOC], BF16) for i in range(2)]
    k_h = [ar2.sbuf_tensor(f"k_h{i}", [128, T], BF16) for i in range(2)]
    vT_h = [ar2.sbuf_tensor(f"vT_h{i}", [128, T], BF16) for i in range(2)]
    v_tm = [ar2.sbuf_tensor(f"v_tm{i}", [128, T], BF16) for i in range(2)]
    sqk = ar2.sbuf_tensor("sqk", [128, T], BF16)
    negk2 = [ar2.sbuf_tensor(f"negk2_{i}", [128, 32], F32) for i in range(2)]
    expk2f = [ar2.sbuf_tensor(f"expk2f_{i}", [128, 32], F32) for i in range(2)]
    expk2b = [ar2.sbuf_tensor(f"expk2b_{i}", [128, 32], BF16) for i in range(2)]
    ptile = [ar2.sbuf_tensor(f"ptile{i}", [128, 512], BF16) for i in range(4)]
    ptm = [ar2.sbuf_tensor(f"ptm{i}", [128, 512], BF16) for i in range(4)]
    rzrow = ar2.sbuf_tensor("rzrow", [1, 512], F32)
    rzbc = ar2.sbuf_tensor("rzbc", [128, 512], F32)
    masks_sb = ar2.sbuf_tensor("masks_sb", [128, 4 * 8 * 512], BF16)
    ar2.allocate()

    t_masks = None
    for l in range(4):
        for kk in range(8):
            t_masks = em.dma("sync",
                             lambda l=l, kk=kk: masks_sb[:, (l * 8 + kk) * 512:(l * 8 + kk + 1) * 512],
                             lambda l=l, kk=kk: masks_in[l, kk, :, :])

    t_q = [None, None]; t_k = [None, None]; t_vt = [None, None]
    vtm_tok = [None, None]
    sqk_tok = None
    nk_tok = [None, None]; ek_tok = [None, None]; eb_tok = [None, None]
    pt_tok = [None] * 4; ptm_tok = [None] * 4
    rz_tok = None; rzbc_tok = None; rz_reader = None
    g_pe = [None, None]
    g_act = [None, None]

    for h in range(NH):
        hb = h % 2
        t_q[hb] = em.dma("sync", lambda hb=hb: q_h[hb][:, :], lambda h=h: qT_d[h, :, :],
                         waits=[t_q[hb], g_pe[hb]])
        t_k[hb] = em.dma("sync", lambda hb=hb: k_h[hb][:, :], lambda h=h: kT_d[h, :, :],
                         waits=[t_k[hb], g_pe[hb]])
        t_vt[hb] = em.dma("sync", lambda hb=hb: vT_h[hb][:, :], lambda h=h: vT_d[h, :, :],
                          waits=[t_vt[hb], g_pe[hb]])
        tk_sq = em.op("vector", lambda e, hb=hb: e.tensor_mul(
            sqk[:, :], k_h[hb][:, :], k_h[hb][:, :]), waits=[t_k[hb], sqk_tok], inc="sDVE")
        kn = psf[4]
        tk_kn = None
        for kb in range(32):
            tk_kn = em.op("tensor", lambda e, kn=kn, kb=kb: e.matmul(
                kn[:, kb:kb + 1], sqk[:, kb * 128:(kb + 1) * 128], ones1b[:, :],
                start=(kb == 0), stop=(kb == 31)),
                waits=[tk_sq, t_ones1b, bank_free[4]] if kb == 0 else [],
                inc="sMM" if kb == 31 else None)
        sqk_tok = tk_kn
        nk_tok[hb] = em.op("vector", lambda e, hb=hb, kn=kn: e.tensor_scalar_mul(
            negk2[hb][:, 0:32], kn[:, 0:32], -AMGM),
            waits=[tk_kn, nk_tok[hb], g_act[hb]], inc="sDVE")
        ek_tok[hb] = em.op("scalar", lambda e, hb=hb, kn=kn: e.activation(
            expk2f[hb][:, 0:32], kn[:, 0:32], AF.Exp, bias=0.0, scale=AMGM),
            waits=[tk_kn, ek_tok[hb]], inc="sACT")
        bank_free[4] = [nk_tok[hb], ek_tok[hb]]
        eb_tok[hb] = em.op("vector", lambda e, hb=hb: e.tensor_copy(
            expk2b[hb][:, :], expk2f[hb][:, :]),
            waits=[ek_tok[hb], eb_tok[hb], g_pe[hb]], inc="sDVE")
        for kb in range(32):
            tk_tr = em.op("tensor", lambda e, hb=hb, kb=kb: e.transpose(
                psb[:, 0:128], vT_h[hb][:, kb * 128:(kb + 1) * 128], idbf[:, :]),
                waits=[t_vt[hb], t_idb, bankb_free[0]], inc="sMM")
            vtm_tok[hb] = em.op("vector", lambda e, hb=hb, kb=kb: e.tensor_scalar(
                v_tm[hb][:, kb * 128:(kb + 1) * 128], psb[:, 0:128],
                expk2f[hb][:, kb:kb + 1], None, ALU.mult),
                waits=[tk_tr, ek_tok[hb], vtm_tok[hb]], inc="sDVE")
            bankb_free[0] = vtm_tok[hb]
        last_te = None
        t_bc = None
        SBANKS = S_BANKS_CFG
        DL = DL_CFG
        s_tok4 = [None] * 4
        for l in range(4):
            ql = l * 512
            cnt = KBCNT[l]
            ob = 2 + ((h * 4 + l) % 2)
            po = psf[ob]
            pz = psf[4]
            t_omm = None; t_zmm = None
            for i in range(32 + DL):
                j = i - DL
                if j >= 0:
                    sb = SBANKS[j % 4]
                    ps = psf[sb]
                    pp = j % 4
                    t_e = em.op("scalar", lambda e, ps=ps, hb=hb, kb=j, pp=pp: e.activation(
                        ptile[pp][:, :], ps[:, :], AF.Exp,
                        bias=negk2[hb][:, kb:kb + 1], scale=SCALE),
                        waits=[s_tok4[j % 4], nk_tok[hb], pt_tok[pp]], inc="sACT")
                    last_te = t_e
                    bank_free[sb] = t_e
                    t_zmm = em.op("tensor", lambda e, pz=pz, hb=hb, pp=pp, kb=j: e.matmul(
                        pz[0:1, :], expk2b[hb][:, kb:kb + 1], ptile[pp][:, :],
                        start=(kb == 0), stop=(kb == 31)),
                        waits=[t_e, eb_tok[hb]] + ([bank_free[4]] if j == 0 else []),
                        inc="sMM" if j == 31 else None)
                    if j < cnt:
                        msk = j - (cnt - 8)
                        if msk >= 0:
                            mp = j % 4
                            t_m = em.op("vector", lambda e, pp=pp, l=l, msk=msk, mp=mp: e.tensor_mul(
                                ptm[mp][:, :], ptile[pp][:, :],
                                masks_sb[:, (l * 8 + msk) * 512:(l * 8 + msk + 1) * 512]),
                                waits=[t_e, t_masks, ptm_tok[mp]], inc="sDVE")
                            src_t = ptm[mp]; wsrc = t_m
                        else:
                            src_t = ptile[pp]; wsrc = t_e
                        t_omm = em.op("tensor", lambda e, po=po, hb=hb, kb=j, src_t=src_t: e.matmul(
                            po[:, :], v_tm[hb][:, kb * 128:(kb + 1) * 128], src_t[:, :],
                            start=(kb == 0), stop=(kb == cnt - 1)),
                            waits=[wsrc, vtm_tok[hb]] + ([bank_free[ob]] if j == 0 else []),
                            inc="sMM" if (msk >= 0 or j == cnt - 1) else None)
                        if msk >= 0:
                            ptm_tok[j % 4] = t_omm
                        pt_tok[pp] = t_omm
                    else:
                        pt_tok[pp] = t_zmm
                if i < 32:
                    sb = SBANKS[i % 4]
                    ps = psf[sb]
                    s_tok4[i % 4] = em.op("tensor", lambda e, ps=ps, hb=hb, kb=i, ql=ql: e.matmul(
                        ps[:, :], k_h[hb][:, kb * 128:(kb + 1) * 128], q_h[hb][:, ql:ql + 512],
                        start=True, stop=True),
                        waits=[t_q[hb], t_k[hb], bank_free[sb]], inc="sMM")
            rz_tok = em.op("vector", lambda e, pz=pz: e.reciprocal(rzrow[0:1, :], pz[0:1, :]),
                           waits=[t_zmm, rz_tok, rz_reader], inc="sDVE")
            t_bc = em.op("tensor", lambda e, pz=pz: e.matmul(
                pz[0:128, :], ones512[0:1, 0:128], rzrow[0:1, :], start=True, stop=True),
                waits=[rz_tok], inc="sMM")
            rz_reader = t_bc
            rzbc_tok = em.op("vector", lambda e, pz=pz: e.tensor_copy(rzbc[:, :], pz[:, :]),
                             waits=[t_bc, rzbc_tok], inc="sDVE")
            bank_free[4] = rzbc_tok
            t_fin = em.op("vector", lambda e, po=po, h=h, ql=ql: e.tensor_mul(
                oT_sb[:, h * TLOC + ql: h * TLOC + ql + 512], po[:, :], rzbc[:, :]),
                waits=[rzbc_tok, t_omm], inc="sDVE")
            bank_free[ob] = t_fin
        g_pe[hb] = t_bc
        g_act[hb] = last_te
    em.barrier()
    ar2.free()
    if dbg == "s2":
        for h in range(NH):
            em.dma("scalar", lambda h=h: oT_dbg[h, :, :],
                   lambda h=h: oT_sb[:, h * TLOC:(h + 1) * TLOC])
        em.op("scalar", lambda e: e.nop(), waits=[("sSC", em.cnt["sSC"])], inc="sACT")
        return nc, em

    ar3 = nc.auto_arena()
    wo_sb = ar3.sbuf_tensor("wo_sb", [128, NFC * D], BF16)
    bo_s = ar3.sbuf_tensor("bo_s", [1, D], F32)
    xres_t = [ar3.sbuf_tensor(f"xres{i}", [128, D], F32) for i in range(2)]
    x1pre = ar3.sbuf_tensor("x1pre", [128, D], F32)
    mean3 = ar3.sbuf_tensor("mean3", [128, 1], F32)
    var3 = ar3.sbuf_tensor("var3", [128, 1], F32)
    rstd3 = ar3.sbuf_tensor("rstd3", [128, 1], F32)
    psum3 = ar3.sbuf_tensor("psum3", [128, 4], F32)
    m23 = ar3.sbuf_tensor("m23", [128, 1], F32)
    xhi = ar3.sbuf_tensor("xhi", [128, D], BF16)
    xnf = ar3.sbuf_tensor("xnf", [128, D], F32)
    xlo = ar3.sbuf_tensor("xlo", [128, D], BF16)
    sthi = ar3.sbuf_tensor("sthi", [128, NFC * 128], BF16)
    stlo = ar3.sbuf_tensor("stlo", [128, NFC * 128], BF16)
    ar3.allocate()

    t_wo = []
    for c in range(NFC):
        t_wo.append(em.dma("gpsimd", lambda c=c: wo_sb[:, c * D:(c + 1) * D],
                           lambda c=c: wo_in[c * 128:(c + 1) * 128, :]))
    t_bo = em.dma("sync", lambda: bo_s[:, :], lambda: bo_in[:, :])
    st3_tok = [None, None]
    tr3_tok = [None, None]
    t_xr = [None, None]
    tok_pre = None; tok_hi = None; tok_lo = None; tok_xn = None
    st_x1 = None; st_hi = None; st_lo = None
    pend3 = [None]
    for tt in range(16):
        xb = tt % 2
        t_xr[xb] = em.dma("sync", lambda xb=xb: xres_t[xb][:, :],
                          lambda tt=tt: xres_in[tt * 128:(tt + 1) * 128, :],
                          waits=[t_xr[xb], tok_pre])
        tmm = [None] * 4
        for dcol in range(4):
            ps = psf[dcol]
            em.op("tensor", lambda e, ps=ps, dcol=dcol: e.matmul(
                ps[:, :], ones512[0:1, 0:128], bo_s[0:1, dcol * 512:(dcol + 1) * 512],
                start=True, stop=False), waits=[bank_free[dcol], t_wo[-1], t_bo, t_ones])
        for c in range(NFC):
            for dcol in range(4):
                ps = psf[dcol]
                tmm[dcol] = em.op("tensor", lambda e, ps=ps, c=c, dcol=dcol, tt=tt: e.matmul(
                    ps[:, :], oT_sb[:, c * TLOC + tt * 128: c * TLOC + tt * 128 + 128],
                    wo_sb[:, c * D + dcol * 512: c * D + dcol * 512 + 512],
                    start=False, stop=(c == NFC - 1)),
                    inc="sMM" if c == NFC - 1 else None)
        if pend3[0] is not None:
            pend3[0]()
        for dcol in range(4):
            ps = psf[dcol]
            tok_pre = em.op("vector", lambda e, ps=ps, dcol=dcol, xb=xb: e.scalar_tensor_tensor(
                x1pre[:, dcol * 512:(dcol + 1) * 512], ps[:, :], 1.0,
                xres_t[xb][:, dcol * 512:(dcol + 1) * 512], ALU.mult, ALU.add,
                accum_out=psum3[:, dcol:dcol + 1]),
                waits=[tmm[dcol], t_xr[xb]] + ([tok_xn, st_x1] if dcol == 0 else []),
                inc="sDVE")
            bank_free[dcol] = tok_pre
        t_mn = em.op("vector", lambda e: e.reduce_sum(mean3[:, :], psum3[:, 0:4], axis=AX.X),
                     waits=[tok_pre], inc="sDVE")
        t_mn = em.op("vector", lambda e: e.tensor_scalar_mul(mean3[:, :], mean3[:, :], 1.0 / D),
                     waits=[t_mn], inc="sDVE")
        t_v = em.op("vector", lambda e: e.scalar_tensor_tensor(
            xnf[:, :], x1pre[:, :], 1.0, x1pre[:, :], ALU.mult, ALU.mult,
            accum_out=var3[:, 0:1]), waits=[tok_pre], inc="sDVE")
        t_m2 = em.op("vector", lambda e: e.tensor_mul(m23[:, :], mean3[:, :], mean3[:, :]),
                     waits=[t_mn], inc="sDVE")
        t_v = em.op("vector", lambda e: e.tensor_scalar(
            var3[:, :], var3[:, :], 1.0 / D, None, ALU.mult), waits=[t_v], inc="sDVE")
        t_v = em.op("vector", lambda e: e.tensor_sub(var3[:, :], var3[:, :], m23[:, :]),
                    waits=[t_v, t_m2], inc="sDVE")
        t_r = em.op("scalar", lambda e: e.activation(
            rstd3[:, :], var3[:, :], AF.Ln, bias=eps_c[:, 0:1], scale=1.0),
            waits=[t_v, t_eps], inc="sACT")
        t_r = em.op("scalar", lambda e: e.activation(
            rstd3[:, :], rstd3[:, :], AF.Exp, bias=0.0, scale=-0.5), waits=[t_r], inc="sACT")
        tok_xn = em.op("vector", lambda e: e.tensor_scalar(
            xnf[:, :], x1pre[:, :], mean3[:, 0:1], rstd3[:, 0:1], ALU.subtract, ALU.mult),
            waits=[t_r], inc="sDVE")
        tok_hi = em.op("vector", lambda e: e.tensor_copy(xhi[:, :], xnf[:, :]),
                       waits=[tok_xn, tok_hi, tr3_tok[0]], inc="sDVE")
        tok_lo = em.op("vector", lambda e: e.tensor_sub(xlo[:, :], xnf[:, :], xhi[:, :]),
                       waits=[tok_hi, tok_lo, tr3_tok[1]], inc="sDVE")
        st_x1 = em.dma("scalar", lambda tt=tt: x1_d[tt * 128:(tt + 1) * 128, :],
                       lambda: xnf[:, :], waits=[tok_xn])

        def mk_pend3(tt=tt, hi_t=tok_hi, lo_t=tok_lo):
            def go():
                tr_hi = None; tr_lo = None
                for c in range(NFC):
                    t_t1 = em.op("tensor", lambda e, c=c: e.transpose(
                        psb[:, 0:128], xhi[:, c * 128:(c + 1) * 128], idbf[:, :]),
                        waits=[hi_t, bankb_free[0]], inc="sMM")
                    tr_hi = em.op("vector", lambda e, c=c: e.tensor_copy(
                        sthi[:, c * 128:(c + 1) * 128], psb[:, 0:128]),
                        waits=[t_t1, st3_tok[0]], inc="sDVE")
                    bankb_free[0] = tr_hi
                    t_t2 = em.op("tensor", lambda e, c=c: e.transpose(
                        psb[:, 0:128], xlo[:, c * 128:(c + 1) * 128], idbf[:, :]),
                        waits=[lo_t, bankb_free[0]], inc="sMM")
                    tr_lo = em.op("vector", lambda e, c=c: e.tensor_copy(
                        stlo[:, c * 128:(c + 1) * 128], psb[:, 0:128]),
                        waits=[t_t2, st3_tok[1]], inc="sDVE")
                    bankb_free[0] = tr_lo
                    tr3_tok[0] = t_t1
                    tr3_tok[1] = t_t2
                for part in range(2):
                    st3_tok[0] = em.dma("scalar", lambda tt=tt, part=part:
                                        x1h_d[part * 8:(part + 1) * 8, :, tt * 128:(tt + 1) * 128]
                                        .rearrange("c p t -> p c t"),
                                        lambda part=part: sthi[:, part * 8 * 128:(part + 1) * 8 * 128]
                                        .rearrange("p (c t) -> p c t", c=8),
                                        waits=[tr_hi])
                    st3_tok[1] = em.dma("scalar", lambda tt=tt, part=part:
                                        x1l_d[part * 8:(part + 1) * 8, :, tt * 128:(tt + 1) * 128]
                                        .rearrange("c p t -> p c t"),
                                        lambda part=part: stlo[:, part * 8 * 128:(part + 1) * 8 * 128]
                                        .rearrange("p (c t) -> p c t", c=8),
                                        waits=[tr_lo])
            return go
        pend3[0] = mk_pend3()
    pend3[0]()
    em.barrier()
    ar3.free()
    ar2o.free()
    if dbg == "s3":
        em.op("scalar", lambda e: e.nop(), waits=[("sSC", em.cnt["sSC"])], inc="sACT")
        return nc, em

    ar4 = nc.auto_arena()
    x1h_sb = ar4.sbuf_tensor("x1h_sb", [128, NFC * 1024], BF16)
    x1l_tt = [ar4.sbuf_tensor(f"x1l_tt{i}", [128, NFC * 128], BF16) for i in range(2)]
    rg_f = ar4.sbuf_tensor("rg_f", [128, NFC * NE], F32)
    rg_hi = ar4.sbuf_tensor("rg_hi", [128, NFC * NE], BF16)
    rg_lo = ar4.sbuf_tensor("rg_lo", [128, NFC * NE], BF16)
    wexp = ar4.sbuf_tensor("wexp", [128, 8 * NE], F32)
    leq1 = ar4.sbuf_tensor("leq1", [128, NE], F32)
    leq2 = ar4.sbuf_tensor("leq2", [128, NE], F32)
    lm1 = ar4.sbuf_tensor("lm1", [128, 1], F32)
    lm2 = ar4.sbuf_tensor("lm2", [128, 1], F32)
    lw1 = ar4.sbuf_tensor("lw1", [128, 1], F32)
    lw2 = ar4.sbuf_tensor("lw2", [128, 1], F32)
    lt1 = ar4.sbuf_tensor("lt1", [128, NE], F32)
    acc = [ar4.sbuf_tensor(f"acc{i}", [128, D], F32) for i in range(8)]
    h_sb = ar4.sbuf_tensor("h_sb", [128, 8 * 1024], BF16)
    wg_sb = [ar4.sbuf_tensor(f"wg_sb{i}", [128, NFC * 128], BF16) for i in range(2)]
    wu_sb = [ar4.sbuf_tensor(f"wu_sb{i}", [128, NFC * 128], BF16) for i in range(2)]
    wd_sb = [ar4.sbuf_tensor(f"wd_sb{i}", [128, 8 * 512], BF16) for i in range(2)]
    sg_sb = [ar4.sbuf_tensor(f"sg_sb{i}", [128, 512], F32) for i in range(2)]
    stout = [ar4.sbuf_tensor(f"stout{i}", [128, D], F32) for i in range(2)]
    mean4 = ar4.sbuf_tensor("mean4", [128, 1], F32)
    var4 = ar4.sbuf_tensor("var4", [128, 1], F32)
    rstd4 = ar4.sbuf_tensor("rstd4", [128, 1], F32)
    tmp4 = ar4.sbuf_tensor("tmp4", [128, D], F32)
    ar4.allocate()

    t_rg = []
    for part in range(4):
        t_rg.append(em.dma("sync",
                           lambda part=part: rg_f[:, part * 4 * NE:(part + 1) * 4 * NE]
                           .rearrange("p (c e) -> p c e", c=4),
                           lambda part=part: rgt_in[part * 512:(part + 1) * 512, :]
                           .rearrange("(c p) e -> p c e", p=128)))
    t_rgh = em.op("vector", lambda e: e.tensor_copy(rg_hi[:, :], rg_f[:, :]),
                  waits=[t_rg[-1]], inc="sDVE")
    t_rgl = em.op("vector", lambda e: e.tensor_sub(rg_lo[:, :], rg_f[:, :], rg_hi[:, :]),
                  waits=[t_rgh], inc="sDVE")

    t_x1h = None
    t_x1l = [None, None]
    x1l_guard = [None, None]
    t_acc = [None] * 8
    t_wgl = [None, None]; t_wul = [None, None]; t_wdl = [None, None]
    wg_guard = [None, None]; wu_guard = [None, None]; wd_guard = [None, None]
    sg_tok = [None, None]
    h_tok = None
    h_guard = None
    x1h_guard = None
    acc_tok = [None] * 8
    st_out = [None, None]
    wx_tok = None
    last_store = None

    for half in range(2):
        hoff = half * 1024
        for part in range(2):
            t_x1h = em.dma("sync", lambda part=part: x1h_sb[:, part * 8 * 1024:(part + 1) * 8 * 1024]
                           .rearrange("p (c t) -> p c t", c=8),
                           lambda hoff=hoff, part=part: x1h_d[part * 8:(part + 1) * 8, :, hoff:hoff + 1024]
                           .rearrange("c p t -> p c t"), waits=[t_x1h, x1h_guard])
        for tt in range(8):
            t_acc[tt] = em.dma("sync", lambda tt=tt: acc[tt][:, :],
                               lambda tt=tt, hoff=hoff: x1_d[hoff + tt * 128: hoff + (tt + 1) * 128, :],
                               waits=[acc_tok[tt], t_acc[tt]])
        for tt in range(8):
            xb = tt % 2
            t_x1l[xb] = em.dma("sync", lambda xb=xb: x1l_tt[xb][:, :]
                               .rearrange("p (c t) -> p c t", c=NFC),
                               lambda tt=tt, hoff=hoff: x1l_d[:, :, hoff + tt * 128: hoff + (tt + 1) * 128]
                               .rearrange("c p t -> p c t"), waits=[t_x1l[xb], x1l_guard[xb]])
            pr = psf[5]
            t_l = None
            for f in range(NFC):
                em.op("tensor", lambda e, pr=pr, f=f, tt=tt: e.matmul(
                    pr[:, 0:NE], x1h_sb[:, f * 1024 + tt * 128: f * 1024 + tt * 128 + 128],
                    rg_hi[:, f * NE:(f + 1) * NE], start=(f == 0), stop=False),
                    waits=[t_x1h, t_rgh] + ([bank_free[5]] if f == 0 else []))
                em.op("tensor", lambda e, pr=pr, f=f, tt=tt: e.matmul(
                    pr[:, 0:NE], x1h_sb[:, f * 1024 + tt * 128: f * 1024 + tt * 128 + 128],
                    rg_lo[:, f * NE:(f + 1) * NE], start=False, stop=False),
                    waits=[t_rgl] if f == 0 else [])
                t_l = em.op("tensor", lambda e, pr=pr, f=f, xb=xb: e.matmul(
                    pr[:, 0:NE], x1l_tt[xb][:, f * 128:(f + 1) * 128],
                    rg_hi[:, f * NE:(f + 1) * NE], start=False, stop=(f == NFC - 1)),
                    waits=[t_x1l[xb]] if f == 0 else [], inc="sMM" if f == NFC - 1 else None)
            x1l_guard[xb] = t_l
            t1 = em.op("vector", lambda e, pr=pr: e.reduce_max(lm1[:, :], pr[:, 0:NE], axis=AX.X),
                       waits=[t_l, wx_tok], inc="sDVE")
            t2 = em.op("vector", lambda e, pr=pr: e.tensor_scalar(
                leq1[:, :], pr[:, 0:NE], lm1[:, 0:1], None, ALU.is_equal), waits=[t1], inc="sDVE")
            t3 = em.op("vector", lambda e: e.tensor_scalar_mul(lt1[:, :], leq1[:, :], -1e30),
                       waits=[t2], inc="sDVE")
            t4 = em.op("vector", lambda e, pr=pr: e.tensor_add(lt1[:, :], lt1[:, :], pr[:, 0:NE]),
                       waits=[t3], inc="sDVE")
            bank_free[5] = t4
            t5 = em.op("vector", lambda e: e.reduce_max(lm2[:, :], lt1[:, :], axis=AX.X),
                       waits=[t4], inc="sDVE")
            t6 = em.op("vector", lambda e: e.tensor_scalar(
                leq2[:, :], lt1[:, :], lm2[:, 0:1], None, ALU.is_equal), waits=[t5], inc="sDVE")
            t7 = em.op("vector", lambda e: e.tensor_scalar(
                lm2[:, :], lm2[:, :], lm1[:, 0:1], None, ALU.subtract), waits=[t6], inc="sDVE")
            t8 = em.op("scalar", lambda e: e.activation(
                lw1[:, :], lm2[:, :], AF.Exp, bias=0.0, scale=1.0), waits=[t7], inc="sACT")
            t9 = em.op("vector", lambda e: e.tensor_scalar_add(lw2[:, :], lw1[:, :], 1.0),
                       waits=[t8], inc="sDVE")
            t10 = em.op("vector", lambda e: e.reciprocal(lw2[:, :], lw2[:, :]),
                        waits=[t9], inc="sDVE")
            t11 = em.op("vector", lambda e: e.tensor_mul(lw1[:, :], lw1[:, :], lw2[:, :]),
                        waits=[t10], inc="sDVE")
            t12 = em.op("vector", lambda e: e.tensor_scalar(
                lt1[:, :], leq1[:, :], lw2[:, 0:1], None, ALU.mult), waits=[t11], inc="sDVE")
            wx_tok = em.op("vector", lambda e, tt=tt: e.scalar_tensor_tensor(
                wexp[:, tt * NE:(tt + 1) * NE], leq2[:, :], lw1[:, 0:1], lt1[:, :],
                ALU.mult, ALU.add), waits=[t12], inc="sDVE")
        for e_i in range(NE):
            for cc in range(8):
                wb = (e_i * 8 + cc) % 2
                t_wgl[wb] = em.dma("gpsimd", lambda wb=wb: wg_sb[wb][:, :]
                                   .rearrange("p (c d) -> p c d", c=NFC),
                                   lambda e_i=e_i, cc=cc: wg_in[e_i, :, cc * 128:(cc + 1) * 128]
                                   .rearrange("(c p) d -> p c d", p=128),
                                   waits=[t_wgl[wb], wg_guard[wb]])
                t_wul[wb] = em.dma("gpsimd", lambda wb=wb: wu_sb[wb][:, :]
                                   .rearrange("p (c d) -> p c d", c=NFC),
                                   lambda e_i=e_i, cc=cc: wu_in[e_i, :, cc * 128:(cc + 1) * 128]
                                   .rearrange("(c p) d -> p c d", p=128),
                                   waits=[t_wul[wb], wu_guard[wb]])
                for tg in range(2):
                    pg = psf[tg]; pu = psf[2 + tg]
                    t_g = None; t_u = None
                    for f in range(NFC):
                        t_g = em.op("tensor", lambda e, pg=pg, f=f, tg=tg, wb=wb: e.matmul(
                            pg[:, :], wg_sb[wb][:, f * 128:(f + 1) * 128],
                            x1h_sb[:, f * 1024 + tg * 512: f * 1024 + tg * 512 + 512],
                            start=(f == 0), stop=(f == NFC - 1)),
                            waits=[t_wgl[wb], t_x1h, bank_free[tg]] if f == 0 else [],
                            inc="sMM" if f == NFC - 1 else None)
                    for f in range(NFC):
                        t_u = em.op("tensor", lambda e, pu=pu, f=f, tg=tg, wb=wb: e.matmul(
                            pu[:, :], wu_sb[wb][:, f * 128:(f + 1) * 128],
                            x1h_sb[:, f * 1024 + tg * 512: f * 1024 + tg * 512 + 512],
                            start=(f == 0), stop=(f == NFC - 1)),
                            waits=[t_wul[wb], bank_free[2 + tg]] if f == 0 else [],
                            inc="sMM" if f == NFC - 1 else None)
                    wg_guard[wb] = t_g
                    wu_guard[wb] = t_u
                    x1h_guard = t_u
                    t_sg = em.op("scalar", lambda e, pg=pg, tg=tg: e.activation(
                        sg_sb[tg][:, :], pg[:, :], AF.Silu, bias=0.0, scale=1.0),
                        waits=[t_g, sg_tok[tg]], inc="sACT")
                    bank_free[tg] = t_sg
                    h_tok = em.op("vector", lambda e, pu=pu, tg=tg, cc=cc: e.tensor_mul(
                        h_sb[:, cc * 1024 + tg * 512: cc * 1024 + tg * 512 + 512],
                        pu[:, :], sg_sb[tg][:, :]),
                        waits=[t_u, t_sg, h_tok, h_guard], inc="sDVE")
                    sg_tok[tg] = h_tok
                    bank_free[2 + tg] = h_tok
            for mcol in range(4):
                wb = (e_i * 4 + mcol) % 2
                t_wdl[wb] = em.dma("gpsimd", lambda wb=wb: wd_sb[wb][:, :]
                                   .rearrange("p (c m) -> p c m", c=8),
                                   lambda e_i=e_i, mcol=mcol: wd_in[e_i, :, mcol * 512:(mcol + 1) * 512]
                                   .rearrange("(c p) m -> p c m", p=128),
                                   waits=[t_wdl[wb], wd_guard[wb]])
                for tt in range(8):
                    pd = psf[4 + (tt % 2)]
                    t_d = None
                    for cc in range(8):
                        t_d = em.op("tensor", lambda e, pd=pd, cc=cc, tt=tt, wb=wb: e.matmul(
                            pd[:, :], h_sb[:, cc * 1024 + tt * 128: cc * 1024 + tt * 128 + 128],
                            wd_sb[wb][:, cc * 512:(cc + 1) * 512],
                            start=(cc == 0), stop=(cc == 7)),
                            waits=[t_wdl[wb], h_tok, bank_free[4 + (tt % 2)]] if cc == 0 else [],
                            inc="sMM" if cc == 7 else None)
                    wd_guard[wb] = t_d
                    h_guard = t_d
                    acc_tok[tt] = em.op("vector", lambda e, pd=pd, tt=tt, mcol=mcol, e_i=e_i:
                                        e.scalar_tensor_tensor(
                                            acc[tt][:, mcol * 512:(mcol + 1) * 512], pd[:, :],
                                            wexp[:, tt * NE + e_i: tt * NE + e_i + 1],
                                            acc[tt][:, mcol * 512:(mcol + 1) * 512],
                                            ALU.mult, ALU.add),
                                        waits=[t_d, wx_tok, t_acc[tt], acc_tok[tt]], inc="sDVE")
                    bank_free[4 + (tt % 2)] = acc_tok[tt]
        for tt in range(8):
            ob = tt % 2
            t_mn = em.op("vector", lambda e, tt=tt: e.reduce_sum(
                mean4[:, :], acc[tt][:, :], axis=AX.X), waits=[acc_tok[tt]], inc="sDVE")
            t_mn = em.op("vector", lambda e: e.tensor_scalar_mul(
                mean4[:, :], mean4[:, :], 1.0 / D), waits=[t_mn], inc="sDVE")
            t_xc = em.op("vector", lambda e, tt=tt: e.tensor_scalar(
                acc[tt][:, :], acc[tt][:, :], mean4[:, 0:1], None, ALU.subtract),
                waits=[t_mn], inc="sDVE")
            t_v = em.op("vector", lambda e, tt=tt: e.scalar_tensor_tensor(
                tmp4[:, :], acc[tt][:, :], 1.0, acc[tt][:, :], ALU.mult, ALU.mult,
                accum_out=var4[:, 0:1]), waits=[t_xc], inc="sDVE")
            t_r = em.op("scalar", lambda e: e.activation(
                rstd4[:, :], var4[:, :], AF.Ln, bias=eps_c[:, 0:1], scale=1.0 / D),
                waits=[t_v, t_eps], inc="sACT")
            t_r = em.op("scalar", lambda e: e.activation(
                rstd4[:, :], rstd4[:, :], AF.Exp, bias=0.0, scale=-0.5), waits=[t_r], inc="sACT")
            t_o = em.op("vector", lambda e, tt=tt, ob=ob: e.tensor_scalar(
                stout[ob][:, :], acc[tt][:, :], rstd4[:, 0:1], None, ALU.mult),
                waits=[t_r, st_out[ob]], inc="sDVE")
            acc_tok[tt] = t_o
            st_out[ob] = em.dma("scalar", lambda tt=tt, hoff=hoff:
                                out_d[hoff + tt * 128: hoff + (tt + 1) * 128, :],
                                lambda ob=ob: stout[ob][:, :], waits=[t_o])
            last_store = st_out[ob]
    em.op("scalar", lambda e: e.nop(), waits=[last_store], inc="sACT")

    return nc, em


def _host_inputs(full):
    x = np.asarray(full["L0_attn_x"], dtype=np.float32)
    wq = np.asarray(full["L0_attn_Wq"], dtype=np.float32)
    wk = np.asarray(full["L0_attn_Wk"], dtype=np.float32)
    wv = np.asarray(full["L0_attn_Wv"], dtype=np.float32)
    wo = np.asarray(full["L0_attn_Wo"], dtype=np.float32)
    bq = np.asarray(full["L0_attn_bq"], dtype=np.float32)
    bk = np.asarray(full["L0_attn_bk"], dtype=np.float32)
    bv = np.asarray(full["L0_attn_bv"], dtype=np.float32)
    bo = np.asarray(full["L0_attn_bo"], dtype=np.float32)
    rg = np.asarray(full["L0_ffn_router_gate"], dtype=np.float32)
    wgu = np.asarray(full["L0_ffn_expert_gate_up"], dtype=np.float32)
    wd = np.asarray(full["L0_ffn_expert_down"], dtype=np.float32)

    perm = np.zeros(D, dtype=np.int64)
    for hh in range(NH):
        base = hh * HS
        perm[base:base + 64] = base + 2 * np.arange(64)
        perm[base + 64:base + 128] = base + 2 * np.arange(64) + 1
    wq_p = np.ascontiguousarray(wq[:, perm])
    wk_p = np.ascontiguousarray(wk[:, perm])
    bq_p = np.ascontiguousarray(bq[perm])[None, :]
    bk_p = np.ascontiguousarray(bk[perm])[None, :]

    inv_freq = 1.0 / (10000.0 ** (np.arange(0, HS, 2, dtype=np.float64) / HS))
    tpos = np.arange(T, dtype=np.float64)
    ang = tpos[None, :] * inv_freq[:, None]
    cosv = np.cos(ang).astype(np.float32)
    sinv = np.sin(ang).astype(np.float32)
    ckf = np.ascontiguousarray(np.concatenate([cosv, cosv], axis=0))
    skf = np.ascontiguousarray(np.concatenate([-sinv, sinv], axis=0))

    swap = np.ascontiguousarray(np.roll(np.eye(128, dtype=np.float32), 64, axis=0))
    id32 = np.eye(128, dtype=np.float32)
    rgt = np.ascontiguousarray(rg.T)
    wg = np.ascontiguousarray(wgu[:, :, 0::2])
    wu = np.ascontiguousarray(wgu[:, :, 1::2])

    ins = []
    for c in range(8):
        b, hf = c // 2, c % 2
        gl = G_OF_HALF[hf]
        cols = np.concatenate([np.arange(512 * g, 512 * g + 512) for g in gl])
        xt = np.ascontiguousarray(x[b].T)
        xtq = np.ascontiguousarray(xt[:, cols])
        xres = np.ascontiguousarray(x[b][cols, :])
        cqf = np.ascontiguousarray(ckf[:, cols])
        sqf = np.ascontiguousarray(skf[:, cols])
        masks = np.zeros((4, 8, 128, 512), dtype=np.float32)
        for l in range(4):
            g = gl[l]
            cnt = KBCNT[l]
            for s in range(8):
                kb = cnt - 8 + s
                jj = kb * 128 + np.arange(128)[:, None]
                qq = 512 * g + np.arange(512)[None, :]
                masks[l, s] = (jj <= qq).astype(np.float32)
        ins.append({
            "xt": xt, "xtq": xtq, "xres": xres,
            "wq": wq_p, "wk": wk_p, "wv": wv, "wo": wo,
            "bq": bq_p, "bk": bk_p, "bv": np.ascontiguousarray(bv[None, :]),
            "bo": np.ascontiguousarray(bo[None, :]),
            "ckf": ckf, "skf": skf, "cqf": cqf, "sqf": sqf,
            "swp": swap, "id32": id32,
            "masks": masks.astype(ml_dtypes.bfloat16),
            "rgt": rgt, "wg": wg, "wu": wu, "wd": wd,
        })
    return ins


def kernel(**inputs):
    ins = _host_inputs(inputs)
    nc, em = build()
    stk = materialize(nc, em)
    try:
        res = run_bass_kernel_spmd(nc, ins, core_ids=list(range(8)))
    finally:
        stk.close()
    out = np.zeros((B, T, D), dtype=np.float32)
    for c in range(8):
        b, hf = c // 2, c % 2
        gl = G_OF_HALF[hf]
        cols = np.concatenate([np.arange(512 * g, 512 * g + 512) for g in gl])
        out[b][cols, :] = res.results[c]["out"]
    return out


# version 07 (final)
# speedup vs baseline: 1.0765x; 1.0436x over previous
import math
import numpy as np
import ml_dtypes
import concourse.bass as bass
import concourse.mybir as mybir
from concourse.bass_utils import run_bass_kernel_spmd

F32 = mybir.dt.float32
BF16 = mybir.dt.bfloat16
AF = mybir.ActivationFunctionType
ALU = mybir.AluOpType
AX = mybir.AxisListType

B, T, D, NH, HS, NE, DI = 4, 4096, 2048, 16, 128, 8, 1024
TLOC = 2048
NFC = D // 128
SCALE = 1.0 / math.sqrt(HS)
AMGM = 0.5 * SCALE
EPS = 1e-5
KBCNT = (8, 16, 24, 32)
G_OF_HALF = ((0, 3, 4, 7), (1, 2, 5, 6))

ENGS = ("sync", "gpsimd", "tensor", "vector", "scalar")
SEMS = ("sSY", "sGP", "sSC", "sMM", "sDVE", "sACT")


def _flat(waits):
    out = []
    for w in waits:
        if w is None:
            continue
        if isinstance(w, list):
            out.extend(x for x in w if x is not None)
        else:
            out.append(w)
    return tuple(out)


class Emit:
    def __init__(self):
        self.ops = {e: [] for e in ENGS}
        self.cnt = {s: 0 for s in SEMS}
        self.last_inc = {e: True for e in ENGS}

    def op(self, eng, fn, waits=(), inc=None, amt=1):
        tok = None
        if inc is not None:
            self.cnt[inc] += amt
            tok = (inc, self.cnt[inc])
        self.ops[eng].append((fn, _flat(waits), (inc, amt) if inc else None))
        self.last_inc[eng] = inc is not None
        return tok

    def dma(self, queue, out_fn, in_fn, waits=()):
        sem = {"sync": "sSY", "gpsimd": "sGP", "scalar": "sSC"}[queue]
        def fn(e, o=out_fn, i=in_fn):
            return e.dma_start(o(), i())
        return self.op(queue, fn, waits=waits, inc=sem, amt=16)

    def barrier(self):
        if not self.last_inc["tensor"]:
            self.op("tensor", lambda e: e.nop(), inc="sMM")
        if not self.last_inc["vector"]:
            self.op("vector", self.dummy_vec, inc="sDVE")
        if not self.last_inc["scalar"]:
            self.op("scalar", self.dummy_act, inc="sACT")
        snap = tuple((s, v) for s, v in self.cnt.items() if v > 0)
        for e in ENGS:
            self.ops[e].append((None, snap, None))


def materialize(nc, em):
    import contextlib
    stk = contextlib.ExitStack()
    sems = {s: stk.enter_context(nc.semaphore(s)) for s in SEMS}
    with nc.Block() as block:
        for eng in ENGS:
            ops = em.ops[eng]

            def mk(ops=ops):
                def f(e):
                    hw = {s: 0 for s in SEMS}
                    for fn, waits, inc in ops:
                        for (s, v) in waits:
                            if v > hw[s]:
                                e.wait_ge(sems[s], v)
                                hw[s] = v
                        if fn is None:
                            continue
                        inst = fn(e)
                        if inc:
                            inst.then_inc(sems[inc[0]], inc[1])
                return f
            getattr(block, eng)(mk())
    return stk


DBG_STAGE = None
S_BANKS_CFG = (0, 1, 5, 6)
DL_CFG = 4


def build():
    dbg = DBG_STAGE
    nc = bass.Bass("TRN2", target_bir_lowering=False)
    em = Emit()

    din = {}
    def I(name, shape, dt=F32):
        din[name] = nc.dram_tensor(name, list(shape), dt, kind="ExternalInput")
        return din[name]

    xt_in = I("xt", (D, T))
    xtq_in = I("xtq", (D, TLOC))
    xres_in = I("xres", (TLOC, D))
    wq_in = I("wq", (D, D)); wk_in = I("wk", (D, D))
    wv_in = I("wv", (D, D)); wo_in = I("wo", (D, D))
    bq_in = I("bq", (1, D)); bk_in = I("bk", (1, D))
    bv_in = I("bv", (1, D)); bo_in = I("bo", (1, D))
    ckf_in = I("ckf", (128, T)); skf_in = I("skf", (128, T))
    cqf_in = I("cqf", (128, TLOC)); sqf_in = I("sqf", (128, TLOC))
    swap_in = I("swp", (128, 128))
    id32_in = I("id32", (128, 128))
    masks_in = I("masks", (4, 8, 128, 512), BF16)
    rgt_in = I("rgt", (D, NE))
    wg_in = I("wg", (NE, D, DI)); wu_in = I("wu", (NE, D, DI))
    wd_in = I("wd", (NE, DI, D))

    out_d = nc.dram_tensor("out", [TLOC, D], F32, kind="ExternalOutput")

    k1 = "ExternalOutput" if dbg == "s1" else "Internal"
    k3 = "ExternalOutput" if dbg == "s3" else "Internal"
    qT_d = nc.dram_tensor("qT_d", [NH, 128, TLOC], BF16, kind=k1)
    kT_d = nc.dram_tensor("kT_d", [NH, 128, T], BF16, kind=k1)
    vT_d = nc.dram_tensor("vT_d", [NH, 128, T], BF16, kind=k1)
    x1_d = nc.dram_tensor("x1_d", [TLOC, D], F32, kind=k3)
    x1h_d = nc.dram_tensor("x1h_d", [NFC, 128, TLOC], BF16, kind=k3)
    x1l_d = nc.dram_tensor("x1l_d", [NFC, 128, TLOC], BF16, kind=k3)
    if dbg == "s2":
        oT_dbg = nc.dram_tensor("oT_dbg", [NH, 128, TLOC], BF16, kind="ExternalOutput")

    import contextlib
    pstk = contextlib.ExitStack()
    psf = [pstk.enter_context(nc.psum_tensor(f"psf{i}", [128, 512], F32)) for i in range(7)]
    psb = pstk.enter_context(nc.psum_tensor("psb", [128, 1024], BF16))
    bank_free = {i: None for i in range(7)}
    bankb_free = [None]

    ar0 = nc.auto_arena()
    ones512 = ar0.sbuf_tensor("ones512", [1, 512], F32)
    ones1b = ar0.sbuf_tensor("ones1b", [128, 1], BF16)
    idbf = ar0.sbuf_tensor("idbfs", [128, 128], BF16)
    swp = ar0.sbuf_tensor("swps", [128, 128], F32)
    dum0 = ar0.sbuf_tensor("dum0", [1, 16], F32)
    eps_c = ar0.sbuf_tensor("eps_c", [128, 1], F32)
    ar0.allocate()
    em.dummy_vec = lambda e: e.memset(dum0[:, :], 0.0)
    em.dummy_act = lambda e: e.copy(dum0[0:1, 0:16], dum0[0:1, 0:16])
    t_eps = em.op("vector", lambda e: e.memset(eps_c[:, :], EPS), inc="sDVE")

    t_ones = em.op("vector", lambda e: e.memset(ones512[:, :], 1.0), inc="sDVE")
    t_ones1b = em.op("vector", lambda e: e.memset(ones1b[:, :], 1.0), inc="sDVE")
    t_idb = em.dma("gpsimd", lambda: idbf[:, :], lambda: id32_in[:, :])
    t_swp = em.dma("sync", lambda: swp[:, :], lambda: swap_in[:, :])

    ar1 = nc.auto_arena()
    xt_sb = ar1.sbuf_tensor("xt_sb", [128, NFC * T], BF16)
    csk = ar1.sbuf_tensor("csk", [128, T], BF16)
    snk = ar1.sbuf_tensor("snk", [128, T], BF16)
    bk_s = ar1.sbuf_tensor("bk_s", [1, D], F32)
    bv_s = ar1.sbuf_tensor("bv_s", [1, D], F32)
    w1_sb = [ar1.sbuf_tensor(f"w1_{i}", [128, NFC * 128], BF16) for i in range(2)]
    qsb1 = [ar1.sbuf_tensor(f"qsb1_{i}", [128, 512], F32) for i in range(2)]
    ta1 = [ar1.sbuf_tensor(f"ta1_{i}", [128, 512], F32) for i in range(2)]
    tb1 = [ar1.sbuf_tensor(f"tb1_{i}", [128, 512], F32) for i in range(2)]
    stg1 = [ar1.sbuf_tensor(f"stg1_{i}", [128, 512], BF16) for i in range(4)]
    ar1.allocate()

    t_xt = []
    for c in range(NFC):
        t_xt.append(em.dma("gpsimd", lambda c=c: xt_sb[:, c * T:(c + 1) * T],
                           lambda c=c: xt_in[c * 128:(c + 1) * 128, :]))
    t_cs = em.dma("gpsimd", lambda: csk[:, :], lambda: ckf_in[:, :])
    t_sn = em.dma("gpsimd", lambda: snk[:, :], lambda: skf_in[:, :])
    t_bk = em.dma("sync", lambda: bk_s[:, :], lambda: bk_in[:, :])
    t_bv = em.dma("sync", lambda: bv_s[:, :], lambda: bv_in[:, :])

    def w_colchunk_dma(w_in, dc, dst, waits):
        def src(w_in=w_in, dc=dc):
            return w_in[:, dc * 128:(dc + 1) * 128].rearrange("(c p) d -> p c d", p=128)
        def dstf(dst=dst):
            return dst[:, :].rearrange("p (c d) -> p c d", c=NFC)
        return em.dma("gpsimd", dstf, src, waits=waits)

    def proj_rope(w_in, b_s, t_b, x_sb, x_tok, t_x_last, cs_sb, sn_sb, t_cst, t_snt,
                  dst_d, n_tg, w_bufs, qsb, ta, tb, stg, pst):
        w_tok = pst["w"]
        w_guard = pst["wg"]
        qs_tok = [None, None]
        ta_tok = [None, None]
        tb_tok = [None, None]
        st_tok = pst["st"]
        pending = [None]
        for dc in range(NH):
            wb = dc % 2
            w_tok[wb] = w_colchunk_dma(w_in, dc, w_bufs[wb], waits=[w_tok[wb], w_guard[wb]])
            for tg in range(n_tg):
                i = (dc * n_tg + tg) % 2
                ps = psf[i]; psw = psf[2 + i]
                em.op("tensor", lambda e, ps=ps, dc=dc, b_s=b_s: e.matmul(
                    ps[:, :], b_s[0:1, dc * 128:(dc + 1) * 128], ones512[0:1, :],
                    start=True, stop=False),
                    waits=[bank_free[i], w_tok[wb], t_ones, t_b, t_x_last])
                tk = None
                for f in range(NFC):
                    last = f == NFC - 1
                    tk = em.op("tensor", lambda e, ps=ps, f=f, tg=tg, wb=wb, x_sb=x_sb,
                               x_tok=x_tok, w_bufs=w_bufs: e.matmul(
                        ps[:, :], w_bufs[wb][:, f * 128:(f + 1) * 128],
                        x_sb[:, f * x_tok + tg * 512: f * x_tok + tg * 512 + 512],
                        start=False, stop=last), inc="sMM" if last else None)
                w_guard[wb] = tk
                qs_tok[i] = em.op("vector", lambda e, qs=qsb[i], ps=ps: e.tensor_copy(
                    qs[:, :], ps[:, :]), waits=[tk, qs_tok[i]], inc="sDVE")
                bank_free[i] = qs_tok[i]

                def mk_pending(i=i, dc=dc, tg=tg, psw=psw, qs_t=qs_tok[i]):
                    def go():
                        tks = em.op("tensor", lambda e, psw=psw, qs=qsb[i]: e.matmul(
                            psw[:, :], swp[:, :], qs[:, :], start=True, stop=True),
                            waits=[qs_t, bank_free[2 + i], t_swp], inc="sMM")
                        col = tg * 512
                        tka = em.op("vector", lambda e, t_=ta[i], qs=qsb[i], col=col, cs=cs_sb:
                                    e.tensor_mul(t_[:, :], qs[:, :], cs[:, col:col + 512]),
                                    waits=[qs_t, t_cst, ta_tok[i]], inc="sDVE")
                        tkb = em.op("vector", lambda e, t_=tb[i], psw=psw, col=col, sn=sn_sb:
                                    e.tensor_mul(t_[:, :], psw[:, :], sn[:, col:col + 512]),
                                    waits=[tks, t_snt, tb_tok[i]], inc="sDVE")
                        bank_free[2 + i] = tkb
                        si = (dc * n_tg + tg) % 4
                        tke = em.op("vector", lambda e, st=stg[si], a=ta[i], b_=tb[i]:
                                    e.tensor_add(st[:, :], a[:, :], b_[:, :]),
                                    waits=[tka, tkb, st_tok[si]], inc="sDVE")
                        ta_tok[i] = tke
                        tb_tok[i] = tke
                        st_tok[si] = em.dma("scalar",
                                            lambda dc=dc, tg=tg, dst_d=dst_d:
                                            dst_d[dc, :, tg * 512:(tg + 1) * 512],
                                            lambda si=si, stg=stg: stg[si][:, :], waits=[tke])
                    return go
                if pending[0] is not None:
                    pending[0]()
                pending[0] = mk_pending()
        pending[0]()
        pending[0] = None
        return st_tok

    def proj_plain(w_in, b_s, t_b, x_sb, x_tok, t_x_last, dst_d, n_tg, w_bufs, stg, pst):
        w_tok = pst["w"]
        w_guard = pst["wg"]
        st_tok = pst["st"]
        for dc in range(NH):
            wb = dc % 2
            w_tok[wb] = w_colchunk_dma(w_in, dc, w_bufs[wb], waits=[w_tok[wb], w_guard[wb]])
            for tg in range(n_tg):
                i = (dc * n_tg + tg) % 2
                ps = psf[i]
                em.op("tensor", lambda e, ps=ps, dc=dc, b_s=b_s: e.matmul(
                    ps[:, :], b_s[0:1, dc * 128:(dc + 1) * 128], ones512[0:1, :],
                    start=True, stop=False),
                    waits=[bank_free[i], w_tok[wb], t_ones, t_b, t_x_last])
                tk = None
                for f in range(NFC):
                    last = f == NFC - 1
                    tk = em.op("tensor", lambda e, ps=ps, f=f, tg=tg, wb=wb: e.matmul(
                        ps[:, :], w_bufs[wb][:, f * 128:(f + 1) * 128],
                        x_sb[:, f * x_tok + tg * 512: f * x_tok + tg * 512 + 512],
                        start=False, stop=last), inc="sMM" if last else None)
                w_guard[wb] = tk
                si = (dc * n_tg + tg) % 4
                tke = em.op("vector", lambda e, st=stg[si], ps=ps: e.tensor_copy(
                    st[:, :], ps[:, :]), waits=[tk, st_tok[si]], inc="sDVE")
                bank_free[i] = tke
                st_tok[si] = em.dma("scalar",
                                    lambda dc=dc, tg=tg: vT_d[dc, :, tg * 512:(tg + 1) * 512],
                                    lambda si=si, stg=stg: stg[si][:, :], waits=[tke])

    pst1 = {"w": [None, None], "wg": [None, None], "st": [None] * 4}
    proj_rope(wk_in, bk_s, t_bk, xt_sb, T, t_xt[-1], csk, snk, t_cs, t_sn,
              kT_d, 8, w1_sb, qsb1, ta1, tb1, stg1, pst1)
    proj_plain(wv_in, bv_s, t_bv, xt_sb, T, t_xt[-1], vT_d, 8, w1_sb, stg1, pst1)
    em.barrier()
    ar1.free()

    ar1b = nc.auto_arena()
    xtq_sb = ar1b.sbuf_tensor("xtq_sb", [128, NFC * TLOC], BF16)
    csq = ar1b.sbuf_tensor("csq", [128, TLOC], BF16)
    snq = ar1b.sbuf_tensor("snq", [128, TLOC], BF16)
    bq_s = ar1b.sbuf_tensor("bq_s", [1, D], F32)
    w1b_sb = [ar1b.sbuf_tensor(f"w1b_{i}", [128, NFC * 128], BF16) for i in range(2)]
    qsb1b = [ar1b.sbuf_tensor(f"qsb1b_{i}", [128, 512], F32) for i in range(2)]
    ta1b = [ar1b.sbuf_tensor(f"ta1b_{i}", [128, 512], F32) for i in range(2)]
    tb1b = [ar1b.sbuf_tensor(f"tb1b_{i}", [128, 512], F32) for i in range(2)]
    stg1b = [ar1b.sbuf_tensor(f"stg1b_{i}", [128, 512], BF16) for i in range(4)]
    ar1b.allocate()

    t_xtq = []
    for c in range(NFC):
        t_xtq.append(em.dma("gpsimd", lambda c=c: xtq_sb[:, c * TLOC:(c + 1) * TLOC],
                            lambda c=c: xtq_in[c * 128:(c + 1) * 128, :]))
    t_csq = em.dma("gpsimd", lambda: csq[:, :], lambda: cqf_in[:, :])
    t_snq = em.dma("gpsimd", lambda: snq[:, :], lambda: sqf_in[:, :])
    t_bq = em.dma("sync", lambda: bq_s[:, :], lambda: bq_in[:, :])
    pst1b = {"w": [None, None], "wg": [None, None], "st": [None] * 4}
    proj_rope(wq_in, bq_s, t_bq, xtq_sb, TLOC, t_xtq[-1], csq, snq, t_csq, t_snq,
              qT_d, 4, w1b_sb, qsb1b, ta1b, tb1b, stg1b, pst1b)
    em.barrier()
    ar1b.free()
    if dbg == "s1":
        em.op("scalar", lambda e: e.nop(), waits=[("sSC", em.cnt["sSC"])], inc="sACT")
        return nc, em

    ar2o = nc.auto_arena()
    oT_sb = ar2o.sbuf_tensor("oT_sb", [128, NH * TLOC], BF16)
    ar2o.allocate()
    ar2 = nc.auto_arena()
    q_h = [ar2.sbuf_tensor(f"q_h{i}", [128, TLOC], BF16) for i in range(2)]
    k_h = [ar2.sbuf_tensor(f"k_h{i}", [128, T], BF16) for i in range(2)]
    vT_h = [ar2.sbuf_tensor(f"vT_h{i}", [128, T], BF16) for i in range(2)]
    v_tm = [ar2.sbuf_tensor(f"v_tm{i}", [128, T], BF16) for i in range(2)]
    sqk = ar2.sbuf_tensor("sqk", [128, T], BF16)
    negk2 = [ar2.sbuf_tensor(f"negk2_{i}", [128, 32], F32) for i in range(2)]
    expk2f = [ar2.sbuf_tensor(f"expk2f_{i}", [128, 32], F32) for i in range(2)]
    expk2b = [ar2.sbuf_tensor(f"expk2b_{i}", [128, 32], BF16) for i in range(2)]
    ptile = [ar2.sbuf_tensor(f"ptile{i}", [128, 512], BF16) for i in range(4)]
    ptm = [ar2.sbuf_tensor(f"ptm{i}", [128, 512], BF16) for i in range(4)]
    rzrow = ar2.sbuf_tensor("rzrow", [1, 512], F32)
    rzbc = ar2.sbuf_tensor("rzbc", [128, 512], F32)
    masks_sb = ar2.sbuf_tensor("masks_sb", [128, 4 * 8 * 512], BF16)
    ar2.allocate()

    t_masks = None
    for l in range(4):
        for kk in range(8):
            t_masks = em.dma("sync",
                             lambda l=l, kk=kk: masks_sb[:, (l * 8 + kk) * 512:(l * 8 + kk + 1) * 512],
                             lambda l=l, kk=kk: masks_in[l, kk, :, :])

    t_q = [None, None]; t_k = [None, None]; t_vt = [None, None]
    vtm_tok = [None, None]
    sqk_tok = None
    nk_tok = [None, None]; ek_tok = [None, None]; eb_tok = [None, None]
    pt_tok = [None] * 4; ptm_tok = [None] * 4
    rz_tok = None; rz_reader = None
    pend_rz = [None]
    rz_state = [None, None, None]
    g_pe = [None, None]
    g_act = [None, None]

    for h in range(NH):
        hb = h % 2
        t_q[hb] = em.dma("sync", lambda hb=hb: q_h[hb][:, :], lambda h=h: qT_d[h, :, :],
                         waits=[t_q[hb], g_pe[hb]])
        t_k[hb] = em.dma("sync", lambda hb=hb: k_h[hb][:, :], lambda h=h: kT_d[h, :, :],
                         waits=[t_k[hb], g_pe[hb]])
        t_vt[hb] = em.dma("sync", lambda hb=hb: vT_h[hb][:, :], lambda h=h: vT_d[h, :, :],
                          waits=[t_vt[hb], g_pe[hb]])
        tk_sq = em.op("vector", lambda e, hb=hb: e.tensor_mul(
            sqk[:, :], k_h[hb][:, :], k_h[hb][:, :]), waits=[t_k[hb], sqk_tok], inc="sDVE")
        kn = psf[4]
        tk_kn = None
        for kb in range(32):
            tk_kn = em.op("tensor", lambda e, kn=kn, kb=kb: e.matmul(
                kn[:, kb:kb + 1], sqk[:, kb * 128:(kb + 1) * 128], ones1b[:, :],
                start=(kb == 0), stop=(kb == 31)),
                waits=[tk_sq, t_ones1b, bank_free[4]] if kb == 0 else [],
                inc="sMM" if kb == 31 else None)
        sqk_tok = tk_kn
        nk_tok[hb] = em.op("vector", lambda e, hb=hb, kn=kn: e.tensor_scalar_mul(
            negk2[hb][:, 0:32], kn[:, 0:32], -AMGM),
            waits=[tk_kn, nk_tok[hb], g_act[hb]], inc="sDVE")
        ek_tok[hb] = em.op("scalar", lambda e, hb=hb, kn=kn: e.activation(
            expk2f[hb][:, 0:32], kn[:, 0:32], AF.Exp, bias=0.0, scale=AMGM),
            waits=[tk_kn, ek_tok[hb]], inc="sACT")
        bank_free[4] = [nk_tok[hb], ek_tok[hb]]
        eb_tok[hb] = em.op("vector", lambda e, hb=hb: e.tensor_copy(
            expk2b[hb][:, :], expk2f[hb][:, :]),
            waits=[ek_tok[hb], eb_tok[hb], g_pe[hb]], inc="sDVE")
        for kb in range(32):
            tk_tr = em.op("tensor", lambda e, hb=hb, kb=kb: e.transpose(
                psb[:, 0:128], vT_h[hb][:, kb * 128:(kb + 1) * 128], idbf[:, :]),
                waits=[t_vt[hb], t_idb, bankb_free[0]], inc="sMM")
            vtm_tok[hb] = em.op("vector", lambda e, hb=hb, kb=kb: e.tensor_scalar(
                v_tm[hb][:, kb * 128:(kb + 1) * 128], psb[:, 0:128],
                expk2f[hb][:, kb:kb + 1], None, ALU.mult),
                waits=[tk_tr, ek_tok[hb], vtm_tok[hb]], inc="sDVE")
            bankb_free[0] = vtm_tok[hb]
        last_te = None
        t_bc = None
        SBANKS = S_BANKS_CFG
        DL = DL_CFG
        s_tok4 = [None] * 4
        for l in range(4):
            ql = l * 512
            cnt = KBCNT[l]
            ob = 2 + ((h * 4 + l) % 2)
            po = psf[ob]
            pz = psf[4]
            t_omm = None; t_zmm = None
            for i in range(32 + DL):
                j = i - DL
                if j >= 0:
                    sb = SBANKS[j % 4]
                    ps = psf[sb]
                    pp = j % 4
                    t_e = em.op("scalar", lambda e, ps=ps, hb=hb, kb=j, pp=pp: e.activation(
                        ptile[pp][:, :], ps[:, :], AF.Exp,
                        bias=negk2[hb][:, kb:kb + 1], scale=SCALE),
                        waits=[s_tok4[j % 4], nk_tok[hb], pt_tok[pp]], inc="sACT")
                    last_te = t_e
                    bank_free[sb] = t_e
                    t_zmm = em.op("tensor", lambda e, pz=pz, hb=hb, pp=pp, kb=j: e.matmul(
                        pz[0:1, :], expk2b[hb][:, kb:kb + 1], ptile[pp][:, :],
                        start=(kb == 0), stop=(kb == 31)),
                        waits=[t_e, eb_tok[hb]] + ([bank_free[4]] if j == 0 else []),
                        inc="sMM" if j == 31 else None)
                    if j < cnt:
                        msk = j - (cnt - 8)
                        if msk >= 0:
                            mp = j % 4
                            t_m = em.op("vector", lambda e, pp=pp, l=l, msk=msk, mp=mp: e.tensor_mul(
                                ptm[mp][:, :], ptile[pp][:, :],
                                masks_sb[:, (l * 8 + msk) * 512:(l * 8 + msk + 1) * 512]),
                                waits=[t_e, t_masks, ptm_tok[mp]], inc="sDVE")
                            src_t = ptm[mp]; wsrc = t_m
                        else:
                            src_t = ptile[pp]; wsrc = t_e
                        t_omm = em.op("tensor", lambda e, po=po, hb=hb, kb=j, src_t=src_t: e.matmul(
                            po[:, :], v_tm[hb][:, kb * 128:(kb + 1) * 128], src_t[:, :],
                            start=(kb == 0), stop=(kb == cnt - 1)),
                            waits=[wsrc, vtm_tok[hb]] + ([bank_free[ob]] if j == 0 else []),
                            inc="sMM" if (msk >= 0 or j == cnt - 1) else None)
                        if msk >= 0:
                            ptm_tok[j % 4] = t_omm
                        pt_tok[pp] = t_omm
                    else:
                        pt_tok[pp] = t_zmm
                if i == 8 and pend_rz[0] is not None:
                    pend_rz[0]()
                    pend_rz[0] = None
                if i < 32:
                    sb = SBANKS[i % 4]
                    ps = psf[sb]
                    s_tok4[i % 4] = em.op("tensor", lambda e, ps=ps, hb=hb, kb=i, ql=ql: e.matmul(
                        ps[:, :], k_h[hb][:, kb * 128:(kb + 1) * 128], q_h[hb][:, ql:ql + 512],
                        start=True, stop=True),
                        waits=[t_q[hb], t_k[hb], bank_free[sb]], inc="sMM")
            rz_tok = em.op("scalar", lambda e, pz=pz: e.activation(
                rzrow[0:1, :], pz[0:1, :], AF.Ln, bias=0.0, scale=1.0),
                waits=[t_zmm, rz_tok, rz_state[0]], inc="sACT")
            bank_free[4] = rz_tok
            rz_tok = em.op("scalar", lambda e: e.activation(
                rzrow[0:1, :], rzrow[0:1, :], AF.Exp, bias=0.0, scale=-1.0),
                waits=[rz_tok], inc="sACT")

            def mk_rz(po=po, h=h, ql=ql, ob=ob, rzt=rz_tok, t_omm=t_omm):
                def go():
                    pb = psf[SBANKS[0]]
                    t_bc = em.op("tensor", lambda e, pb=pb: e.matmul(
                        pb[0:128, :], ones512[0:1, 0:128], rzrow[0:1, :], start=True, stop=True),
                        waits=[rzt, bank_free[SBANKS[0]]], inc="sMM")
                    rz_state[0] = t_bc
                    tk = em.op("vector", lambda e, pb=pb: e.tensor_copy(rzbc[:, :], pb[:, :]),
                               waits=[t_bc, rz_state[1]], inc="sDVE")
                    rz_state[1] = tk
                    bank_free[SBANKS[0]] = tk
                    t_fin = em.op("vector", lambda e, po=po, h=h, ql=ql: e.tensor_mul(
                        oT_sb[:, h * TLOC + ql: h * TLOC + ql + 512], po[:, :], rzbc[:, :]),
                        waits=[tk, t_omm], inc="sDVE")
                    bank_free[ob] = t_fin
                    rz_state[2] = t_fin
                return go
            pend_rz[0] = mk_rz()
        g_pe[hb] = t_zmm
        g_act[hb] = last_te
    pend_rz[0]()
    pend_rz[0] = None
    em.barrier()
    ar2.free()
    if dbg == "s2":
        for h in range(NH):
            em.dma("scalar", lambda h=h: oT_dbg[h, :, :],
                   lambda h=h: oT_sb[:, h * TLOC:(h + 1) * TLOC])
        em.op("scalar", lambda e: e.nop(), waits=[("sSC", em.cnt["sSC"])], inc="sACT")
        return nc, em

    ar3 = nc.auto_arena()
    wo_sb = ar3.sbuf_tensor("wo_sb", [128, NFC * D], BF16)
    bo_s = ar3.sbuf_tensor("bo_s", [1, D], F32)
    xres_t = [ar3.sbuf_tensor(f"xres{i}", [128, D], F32) for i in range(2)]
    x1pre = ar3.sbuf_tensor("x1pre", [128, D], F32)
    mean3 = ar3.sbuf_tensor("mean3", [128, 1], F32)
    var3 = ar3.sbuf_tensor("var3", [128, 1], F32)
    rstd3 = ar3.sbuf_tensor("rstd3", [128, 1], F32)
    psum3 = ar3.sbuf_tensor("psum3", [128, 4], F32)
    m23 = ar3.sbuf_tensor("m23", [128, 1], F32)
    xhi = ar3.sbuf_tensor("xhi", [128, D], BF16)
    xnf = ar3.sbuf_tensor("xnf", [128, D], F32)
    xlo = ar3.sbuf_tensor("xlo", [128, D], BF16)
    sthi = ar3.sbuf_tensor("sthi", [128, NFC * 128], BF16)
    stlo = ar3.sbuf_tensor("stlo", [128, NFC * 128], BF16)
    ar3.allocate()

    t_wo = []
    for c in range(NFC):
        t_wo.append(em.dma("gpsimd", lambda c=c: wo_sb[:, c * D:(c + 1) * D],
                           lambda c=c: wo_in[c * 128:(c + 1) * 128, :]))
    t_bo = em.dma("sync", lambda: bo_s[:, :], lambda: bo_in[:, :])
    st3_tok = [None, None]
    tr3_tok = [None, None]
    t_xr = [None, None]
    tok_pre = None; tok_hi = None; tok_lo = None; tok_xn = None
    st_x1 = None; st_hi = None; st_lo = None
    pend3 = [None]
    for tt in range(16):
        xb = tt % 2
        t_xr[xb] = em.dma("sync", lambda xb=xb: xres_t[xb][:, :],
                          lambda tt=tt: xres_in[tt * 128:(tt + 1) * 128, :],
                          waits=[t_xr[xb], tok_pre])
        tmm = [None] * 4
        for dcol in range(4):
            ps = psf[dcol]
            em.op("tensor", lambda e, ps=ps, dcol=dcol: e.matmul(
                ps[:, :], ones512[0:1, 0:128], bo_s[0:1, dcol * 512:(dcol + 1) * 512],
                start=True, stop=False), waits=[bank_free[dcol], t_bo, t_ones])
        for c in range(NFC):
            for dcol in range(4):
                ps = psf[dcol]
                tmm[dcol] = em.op("tensor", lambda e, ps=ps, c=c, dcol=dcol, tt=tt: e.matmul(
                    ps[:, :], oT_sb[:, c * TLOC + tt * 128: c * TLOC + tt * 128 + 128],
                    wo_sb[:, c * D + dcol * 512: c * D + dcol * 512 + 512],
                    start=False, stop=(c == NFC - 1)),
                    waits=[t_wo[c]] if (tt == 0 and dcol == 0) else [],
                    inc="sMM" if c == NFC - 1 else None)
        if pend3[0] is not None:
            pend3[0]()
        for dcol in range(4):
            ps = psf[dcol]
            tok_pre = em.op("vector", lambda e, ps=ps, dcol=dcol, xb=xb: e.scalar_tensor_tensor(
                x1pre[:, dcol * 512:(dcol + 1) * 512], ps[:, :], 1.0,
                xres_t[xb][:, dcol * 512:(dcol + 1) * 512], ALU.mult, ALU.add,
                accum_out=psum3[:, dcol:dcol + 1]),
                waits=[tmm[dcol], t_xr[xb]] + ([tok_xn, st_x1] if dcol == 0 else []),
                inc="sDVE")
            bank_free[dcol] = tok_pre
        t_mn = em.op("vector", lambda e: e.reduce_sum(mean3[:, :], psum3[:, 0:4], axis=AX.X),
                     waits=[tok_pre], inc="sDVE")
        t_mn = em.op("vector", lambda e: e.tensor_scalar_mul(mean3[:, :], mean3[:, :], 1.0 / D),
                     waits=[t_mn], inc="sDVE")
        t_v = em.op("vector", lambda e: e.scalar_tensor_tensor(
            xnf[:, :], x1pre[:, :], 1.0, x1pre[:, :], ALU.mult, ALU.mult,
            accum_out=var3[:, 0:1]), waits=[tok_pre], inc="sDVE")
        t_m2 = em.op("vector", lambda e: e.tensor_mul(m23[:, :], mean3[:, :], mean3[:, :]),
                     waits=[t_mn], inc="sDVE")
        t_v = em.op("vector", lambda e: e.tensor_scalar(
            var3[:, :], var3[:, :], 1.0 / D, None, ALU.mult), waits=[t_v], inc="sDVE")
        t_v = em.op("vector", lambda e: e.tensor_sub(var3[:, :], var3[:, :], m23[:, :]),
                    waits=[t_v, t_m2], inc="sDVE")
        t_r = em.op("scalar", lambda e: e.activation(
            rstd3[:, :], var3[:, :], AF.Ln, bias=eps_c[:, 0:1], scale=1.0),
            waits=[t_v, t_eps], inc="sACT")
        t_r = em.op("scalar", lambda e: e.activation(
            rstd3[:, :], rstd3[:, :], AF.Exp, bias=0.0, scale=-0.5), waits=[t_r], inc="sACT")
        tok_xn = em.op("vector", lambda e: e.tensor_scalar(
            xnf[:, :], x1pre[:, :], mean3[:, 0:1], rstd3[:, 0:1], ALU.subtract, ALU.mult),
            waits=[t_r], inc="sDVE")
        tok_hi = em.op("vector", lambda e: e.tensor_copy(xhi[:, :], xnf[:, :]),
                       waits=[tok_xn, tok_hi, tr3_tok[0]], inc="sDVE")
        tok_lo = em.op("vector", lambda e: e.tensor_sub(xlo[:, :], xnf[:, :], xhi[:, :]),
                       waits=[tok_hi, tok_lo, tr3_tok[1]], inc="sDVE")
        st_x1 = em.dma("scalar", lambda tt=tt: x1_d[tt * 128:(tt + 1) * 128, :],
                       lambda: xnf[:, :], waits=[tok_xn])

        def mk_pend3(tt=tt, hi_t=tok_hi, lo_t=tok_lo):
            def go():
                tr_hi = None; tr_lo = None
                for c in range(NFC):
                    t_t1 = em.op("tensor", lambda e, c=c: e.transpose(
                        psb[:, 0:128], xhi[:, c * 128:(c + 1) * 128], idbf[:, :]),
                        waits=[hi_t, bankb_free[0]], inc="sMM")
                    tr_hi = em.op("vector", lambda e, c=c: e.tensor_copy(
                        sthi[:, c * 128:(c + 1) * 128], psb[:, 0:128]),
                        waits=[t_t1, st3_tok[0]], inc="sDVE")
                    bankb_free[0] = tr_hi
                    t_t2 = em.op("tensor", lambda e, c=c: e.transpose(
                        psb[:, 0:128], xlo[:, c * 128:(c + 1) * 128], idbf[:, :]),
                        waits=[lo_t, bankb_free[0]], inc="sMM")
                    tr_lo = em.op("vector", lambda e, c=c: e.tensor_copy(
                        stlo[:, c * 128:(c + 1) * 128], psb[:, 0:128]),
                        waits=[t_t2, st3_tok[1]], inc="sDVE")
                    bankb_free[0] = tr_lo
                    tr3_tok[0] = t_t1
                    tr3_tok[1] = t_t2
                for part in range(2):
                    st3_tok[0] = em.dma("scalar", lambda tt=tt, part=part:
                                        x1h_d[part * 8:(part + 1) * 8, :, tt * 128:(tt + 1) * 128]
                                        .rearrange("c p t -> p c t"),
                                        lambda part=part: sthi[:, part * 8 * 128:(part + 1) * 8 * 128]
                                        .rearrange("p (c t) -> p c t", c=8),
                                        waits=[tr_hi])
                    st3_tok[1] = em.dma("scalar", lambda tt=tt, part=part:
                                        x1l_d[part * 8:(part + 1) * 8, :, tt * 128:(tt + 1) * 128]
                                        .rearrange("c p t -> p c t"),
                                        lambda part=part: stlo[:, part * 8 * 128:(part + 1) * 8 * 128]
                                        .rearrange("p (c t) -> p c t", c=8),
                                        waits=[tr_lo])
            return go
        pend3[0] = mk_pend3()
    pend3[0]()
    em.barrier()
    ar3.free()
    ar2o.free()
    if dbg == "s3":
        em.op("scalar", lambda e: e.nop(), waits=[("sSC", em.cnt["sSC"])], inc="sACT")
        return nc, em

    ar4 = nc.auto_arena()
    x1h_sb = ar4.sbuf_tensor("x1h_sb", [128, NFC * 1024], BF16)
    x1l_tt = [ar4.sbuf_tensor(f"x1l_tt{i}", [128, NFC * 128], BF16) for i in range(2)]
    rg_f = ar4.sbuf_tensor("rg_f", [128, NFC * NE], F32)
    rg_hi = ar4.sbuf_tensor("rg_hi", [128, NFC * NE], BF16)
    rg_lo = ar4.sbuf_tensor("rg_lo", [128, NFC * NE], BF16)
    wexp = ar4.sbuf_tensor("wexp", [128, 8 * NE], F32)
    leq1 = ar4.sbuf_tensor("leq1", [128, NE], F32)
    leq2 = ar4.sbuf_tensor("leq2", [128, NE], F32)
    lm1 = ar4.sbuf_tensor("lm1", [128, 1], F32)
    lm2 = ar4.sbuf_tensor("lm2", [128, 1], F32)
    lw1 = ar4.sbuf_tensor("lw1", [128, 1], F32)
    lw2 = ar4.sbuf_tensor("lw2", [128, 1], F32)
    lt1 = ar4.sbuf_tensor("lt1", [128, NE], F32)
    acc = [ar4.sbuf_tensor(f"acc{i}", [128, D], F32) for i in range(8)]
    h_sb = ar4.sbuf_tensor("h_sb", [128, 8 * 1024], BF16)
    wg_sb = [ar4.sbuf_tensor(f"wg_sb{i}", [128, NFC * 128], BF16) for i in range(2)]
    wu_sb = [ar4.sbuf_tensor(f"wu_sb{i}", [128, NFC * 128], BF16) for i in range(2)]
    wd_sb = [ar4.sbuf_tensor(f"wd_sb{i}", [128, 8 * 512], BF16) for i in range(2)]
    sg_sb = [ar4.sbuf_tensor(f"sg_sb{i}", [128, 512], F32) for i in range(2)]
    stout = [ar4.sbuf_tensor(f"stout{i}", [128, D], F32) for i in range(2)]
    mean4 = ar4.sbuf_tensor("mean4", [128, 1], F32)
    var4 = ar4.sbuf_tensor("var4", [128, 1], F32)
    rstd4 = ar4.sbuf_tensor("rstd4", [128, 1], F32)
    tmp4 = ar4.sbuf_tensor("tmp4", [128, D], F32)
    ar4.allocate()

    t_rg = []
    for part in range(4):
        t_rg.append(em.dma("sync",
                           lambda part=part: rg_f[:, part * 4 * NE:(part + 1) * 4 * NE]
                           .rearrange("p (c e) -> p c e", c=4),
                           lambda part=part: rgt_in[part * 512:(part + 1) * 512, :]
                           .rearrange("(c p) e -> p c e", p=128)))
    t_rgh = em.op("vector", lambda e: e.tensor_copy(rg_hi[:, :], rg_f[:, :]),
                  waits=[t_rg[-1]], inc="sDVE")
    t_rgl = em.op("vector", lambda e: e.tensor_sub(rg_lo[:, :], rg_f[:, :], rg_hi[:, :]),
                  waits=[t_rgh], inc="sDVE")

    t_x1h = None
    t_x1hp = [None, None]
    t_x1l = [None, None]
    x1l_guard = [None, None]
    t_acc = [None] * 8
    t_wgl = [None, None]; t_wul = [None, None]; t_wdl = [None, None]
    wg_guard = [None, None]; wu_guard = [None, None]; wd_guard = [None, None]
    sg_tok = [None, None]
    h_tok = None
    h_guard = None
    x1h_guard = None
    acc_tok = [None] * 8
    st_out = [None, None]
    wx_tok = None
    last_store = None

    for half in range(2):
        hoff = half * 1024
        for part in range(2):
            t_x1hp[part] = em.dma("sync", lambda part=part: x1h_sb[:, part * 8 * 1024:(part + 1) * 8 * 1024]
                                  .rearrange("p (c t) -> p c t", c=8),
                                  lambda hoff=hoff, part=part: x1h_d[part * 8:(part + 1) * 8, :, hoff:hoff + 1024]
                                  .rearrange("c p t -> p c t"), waits=[t_x1hp[part], x1h_guard])
        t_x1h = t_x1hp[1]
        for tt in range(8):
            t_acc[tt] = em.dma("sync", lambda tt=tt: acc[tt][:, :],
                               lambda tt=tt, hoff=hoff: x1_d[hoff + tt * 128: hoff + (tt + 1) * 128, :],
                               waits=[acc_tok[tt], t_acc[tt]])
        for tt in range(8):
            xb = tt % 2
            t_x1l[xb] = em.dma("sync", lambda xb=xb: x1l_tt[xb][:, :]
                               .rearrange("p (c t) -> p c t", c=NFC),
                               lambda tt=tt, hoff=hoff: x1l_d[:, :, hoff + tt * 128: hoff + (tt + 1) * 128]
                               .rearrange("c p t -> p c t"), waits=[t_x1l[xb], x1l_guard[xb]])
            pr = psf[5]
            t_l = None
            for f in range(NFC):
                em.op("tensor", lambda e, pr=pr, f=f, tt=tt: e.matmul(
                    pr[:, 0:NE], x1h_sb[:, f * 1024 + tt * 128: f * 1024 + tt * 128 + 128],
                    rg_hi[:, f * NE:(f + 1) * NE], start=(f == 0), stop=False),
                    waits=[t_x1hp[f // 8], t_rgh] + ([bank_free[5]] if f == 0 else []))
                em.op("tensor", lambda e, pr=pr, f=f, tt=tt: e.matmul(
                    pr[:, 0:NE], x1h_sb[:, f * 1024 + tt * 128: f * 1024 + tt * 128 + 128],
                    rg_lo[:, f * NE:(f + 1) * NE], start=False, stop=False),
                    waits=[t_rgl] if f == 0 else [])
                t_l = em.op("tensor", lambda e, pr=pr, f=f, xb=xb: e.matmul(
                    pr[:, 0:NE], x1l_tt[xb][:, f * 128:(f + 1) * 128],
                    rg_hi[:, f * NE:(f + 1) * NE], start=False, stop=(f == NFC - 1)),
                    waits=[t_x1l[xb]] if f == 0 else [], inc="sMM" if f == NFC - 1 else None)
            x1l_guard[xb] = t_l
            t1 = em.op("vector", lambda e, pr=pr: e.reduce_max(lm1[:, :], pr[:, 0:NE], axis=AX.X),
                       waits=[t_l, wx_tok], inc="sDVE")
            t2 = em.op("vector", lambda e, pr=pr: e.tensor_scalar(
                leq1[:, :], pr[:, 0:NE], lm1[:, 0:1], None, ALU.is_equal), waits=[t1], inc="sDVE")
            t3 = em.op("vector", lambda e: e.tensor_scalar_mul(lt1[:, :], leq1[:, :], -1e30),
                       waits=[t2], inc="sDVE")
            t4 = em.op("vector", lambda e, pr=pr: e.tensor_add(lt1[:, :], lt1[:, :], pr[:, 0:NE]),
                       waits=[t3], inc="sDVE")
            bank_free[5] = t4
            t5 = em.op("vector", lambda e: e.reduce_max(lm2[:, :], lt1[:, :], axis=AX.X),
                       waits=[t4], inc="sDVE")
            t6 = em.op("vector", lambda e: e.tensor_scalar(
                leq2[:, :], lt1[:, :], lm2[:, 0:1], None, ALU.is_equal), waits=[t5], inc="sDVE")
            t7 = em.op("vector", lambda e: e.tensor_scalar(
                lm2[:, :], lm2[:, :], lm1[:, 0:1], None, ALU.subtract), waits=[t6], inc="sDVE")
            t8 = em.op("scalar", lambda e: e.activation(
                lw1[:, :], lm2[:, :], AF.Exp, bias=0.0, scale=1.0), waits=[t7], inc="sACT")
            t9 = em.op("vector", lambda e: e.tensor_scalar_add(lw2[:, :], lw1[:, :], 1.0),
                       waits=[t8], inc="sDVE")
            t10 = em.op("vector", lambda e: e.reciprocal(lw2[:, :], lw2[:, :]),
                        waits=[t9], inc="sDVE")
            t11 = em.op("vector", lambda e: e.tensor_mul(lw1[:, :], lw1[:, :], lw2[:, :]),
                        waits=[t10], inc="sDVE")
            t12 = em.op("vector", lambda e: e.tensor_scalar(
                lt1[:, :], leq1[:, :], lw2[:, 0:1], None, ALU.mult), waits=[t11], inc="sDVE")
            wx_tok = em.op("vector", lambda e, tt=tt: e.scalar_tensor_tensor(
                wexp[:, tt * NE:(tt + 1) * NE], leq2[:, :], lw1[:, 0:1], lt1[:, :],
                ALU.mult, ALU.add), waits=[t12], inc="sDVE")
        for e_i in range(NE):
            for cc in range(8):
                wb = (e_i * 8 + cc) % 2
                t_wgl[wb] = em.dma("gpsimd", lambda wb=wb: wg_sb[wb][:, :]
                                   .rearrange("p (c d) -> p c d", c=NFC),
                                   lambda e_i=e_i, cc=cc: wg_in[e_i, :, cc * 128:(cc + 1) * 128]
                                   .rearrange("(c p) d -> p c d", p=128),
                                   waits=[t_wgl[wb], wg_guard[wb]])
                t_wul[wb] = em.dma("gpsimd", lambda wb=wb: wu_sb[wb][:, :]
                                   .rearrange("p (c d) -> p c d", c=NFC),
                                   lambda e_i=e_i, cc=cc: wu_in[e_i, :, cc * 128:(cc + 1) * 128]
                                   .rearrange("(c p) d -> p c d", p=128),
                                   waits=[t_wul[wb], wu_guard[wb]])
                for tg in range(2):
                    pg = psf[tg]; pu = psf[2 + tg]
                    t_g = None; t_u = None
                    for f in range(NFC):
                        t_g = em.op("tensor", lambda e, pg=pg, f=f, tg=tg, wb=wb: e.matmul(
                            pg[:, :], wg_sb[wb][:, f * 128:(f + 1) * 128],
                            x1h_sb[:, f * 1024 + tg * 512: f * 1024 + tg * 512 + 512],
                            start=(f == 0), stop=(f == NFC - 1)),
                            waits=[t_wgl[wb], t_x1hp[f // 8], bank_free[tg]] if f == 0 else [t_x1hp[f // 8]],
                            inc="sMM" if f == NFC - 1 else None)
                    for f in range(NFC):
                        t_u = em.op("tensor", lambda e, pu=pu, f=f, tg=tg, wb=wb: e.matmul(
                            pu[:, :], wu_sb[wb][:, f * 128:(f + 1) * 128],
                            x1h_sb[:, f * 1024 + tg * 512: f * 1024 + tg * 512 + 512],
                            start=(f == 0), stop=(f == NFC - 1)),
                            waits=[t_wul[wb], bank_free[2 + tg]] if f == 0 else [],
                            inc="sMM" if f == NFC - 1 else None)
                    wg_guard[wb] = t_g
                    wu_guard[wb] = t_u
                    x1h_guard = t_u
                    t_sg = em.op("scalar", lambda e, pg=pg, tg=tg: e.activation(
                        sg_sb[tg][:, :], pg[:, :], AF.Silu, bias=0.0, scale=1.0),
                        waits=[t_g, sg_tok[tg]], inc="sACT")
                    bank_free[tg] = t_sg
                    h_tok = em.op("vector", lambda e, pu=pu, tg=tg, cc=cc: e.tensor_mul(
                        h_sb[:, cc * 1024 + tg * 512: cc * 1024 + tg * 512 + 512],
                        pu[:, :], sg_sb[tg][:, :]),
                        waits=[t_u, t_sg, h_tok, h_guard], inc="sDVE")
                    sg_tok[tg] = h_tok
                    bank_free[2 + tg] = h_tok
            for mcol in range(4):
                wb = (e_i * 4 + mcol) % 2
                t_wdl[wb] = em.dma("gpsimd", lambda wb=wb: wd_sb[wb][:, :]
                                   .rearrange("p (c m) -> p c m", c=8),
                                   lambda e_i=e_i, mcol=mcol: wd_in[e_i, :, mcol * 512:(mcol + 1) * 512]
                                   .rearrange("(c p) m -> p c m", p=128),
                                   waits=[t_wdl[wb], wd_guard[wb]])
                for tt in range(8):
                    pd = psf[4 + (tt % 2)]
                    t_d = None
                    for cc in range(8):
                        t_d = em.op("tensor", lambda e, pd=pd, cc=cc, tt=tt, wb=wb: e.matmul(
                            pd[:, :], h_sb[:, cc * 1024 + tt * 128: cc * 1024 + tt * 128 + 128],
                            wd_sb[wb][:, cc * 512:(cc + 1) * 512],
                            start=(cc == 0), stop=(cc == 7)),
                            waits=[t_wdl[wb], h_tok, bank_free[4 + (tt % 2)]] if cc == 0 else [],
                            inc="sMM" if cc == 7 else None)
                    wd_guard[wb] = t_d
                    h_guard = t_d
                    acc_tok[tt] = em.op("vector", lambda e, pd=pd, tt=tt, mcol=mcol, e_i=e_i:
                                        e.scalar_tensor_tensor(
                                            acc[tt][:, mcol * 512:(mcol + 1) * 512], pd[:, :],
                                            wexp[:, tt * NE + e_i: tt * NE + e_i + 1],
                                            acc[tt][:, mcol * 512:(mcol + 1) * 512],
                                            ALU.mult, ALU.add),
                                        waits=[t_d, wx_tok, t_acc[tt], acc_tok[tt]], inc="sDVE")
                    bank_free[4 + (tt % 2)] = acc_tok[tt]
        for tt in range(8):
            ob = tt % 2
            t_mn = em.op("vector", lambda e, tt=tt: e.reduce_sum(
                mean4[:, :], acc[tt][:, :], axis=AX.X), waits=[acc_tok[tt]], inc="sDVE")
            t_mn = em.op("vector", lambda e: e.tensor_scalar_mul(
                mean4[:, :], mean4[:, :], 1.0 / D), waits=[t_mn], inc="sDVE")
            t_xc = em.op("vector", lambda e, tt=tt: e.tensor_scalar(
                acc[tt][:, :], acc[tt][:, :], mean4[:, 0:1], None, ALU.subtract),
                waits=[t_mn], inc="sDVE")
            t_v = em.op("vector", lambda e, tt=tt: e.scalar_tensor_tensor(
                tmp4[:, :], acc[tt][:, :], 1.0, acc[tt][:, :], ALU.mult, ALU.mult,
                accum_out=var4[:, 0:1]), waits=[t_xc], inc="sDVE")
            t_r = em.op("scalar", lambda e: e.activation(
                rstd4[:, :], var4[:, :], AF.Ln, bias=eps_c[:, 0:1], scale=1.0 / D),
                waits=[t_v, t_eps], inc="sACT")
            t_r = em.op("scalar", lambda e: e.activation(
                rstd4[:, :], rstd4[:, :], AF.Exp, bias=0.0, scale=-0.5), waits=[t_r], inc="sACT")
            t_o = em.op("vector", lambda e, tt=tt, ob=ob: e.tensor_scalar(
                stout[ob][:, :], acc[tt][:, :], rstd4[:, 0:1], None, ALU.mult),
                waits=[t_r, st_out[ob]], inc="sDVE")
            acc_tok[tt] = t_o
            st_out[ob] = em.dma("scalar", lambda tt=tt, hoff=hoff:
                                out_d[hoff + tt * 128: hoff + (tt + 1) * 128, :],
                                lambda ob=ob: stout[ob][:, :], waits=[t_o])
            last_store = st_out[ob]
    em.op("scalar", lambda e: e.nop(), waits=[last_store], inc="sACT")

    return nc, em


def _host_inputs(full):
    x = np.asarray(full["L0_attn_x"], dtype=np.float32)
    wq = np.asarray(full["L0_attn_Wq"], dtype=np.float32)
    wk = np.asarray(full["L0_attn_Wk"], dtype=np.float32)
    wv = np.asarray(full["L0_attn_Wv"], dtype=np.float32)
    wo = np.asarray(full["L0_attn_Wo"], dtype=np.float32)
    bq = np.asarray(full["L0_attn_bq"], dtype=np.float32)
    bk = np.asarray(full["L0_attn_bk"], dtype=np.float32)
    bv = np.asarray(full["L0_attn_bv"], dtype=np.float32)
    bo = np.asarray(full["L0_attn_bo"], dtype=np.float32)
    rg = np.asarray(full["L0_ffn_router_gate"], dtype=np.float32)
    wgu = np.asarray(full["L0_ffn_expert_gate_up"], dtype=np.float32)
    wd = np.asarray(full["L0_ffn_expert_down"], dtype=np.float32)

    perm = np.zeros(D, dtype=np.int64)
    for hh in range(NH):
        base = hh * HS
        perm[base:base + 64] = base + 2 * np.arange(64)
        perm[base + 64:base + 128] = base + 2 * np.arange(64) + 1
    wq_p = np.ascontiguousarray(wq[:, perm])
    wk_p = np.ascontiguousarray(wk[:, perm])
    bq_p = np.ascontiguousarray(bq[perm])[None, :]
    bk_p = np.ascontiguousarray(bk[perm])[None, :]

    inv_freq = 1.0 / (10000.0 ** (np.arange(0, HS, 2, dtype=np.float64) / HS))
    tpos = np.arange(T, dtype=np.float64)
    ang = tpos[None, :] * inv_freq[:, None]
    cosv = np.cos(ang).astype(np.float32)
    sinv = np.sin(ang).astype(np.float32)
    ckf = np.ascontiguousarray(np.concatenate([cosv, cosv], axis=0))
    skf = np.ascontiguousarray(np.concatenate([-sinv, sinv], axis=0))

    swap = np.ascontiguousarray(np.roll(np.eye(128, dtype=np.float32), 64, axis=0))
    id32 = np.eye(128, dtype=np.float32)
    rgt = np.ascontiguousarray(rg.T)
    wg = np.ascontiguousarray(wgu[:, :, 0::2])
    wu = np.ascontiguousarray(wgu[:, :, 1::2])

    ins = []
    for c in range(8):
        b, hf = c // 2, c % 2
        gl = G_OF_HALF[hf]
        cols = np.concatenate([np.arange(512 * g, 512 * g + 512) for g in gl])
        xt = np.ascontiguousarray(x[b].T)
        xtq = np.ascontiguousarray(xt[:, cols])
        xres = np.ascontiguousarray(x[b][cols, :])
        cqf = np.ascontiguousarray(ckf[:, cols])
        sqf = np.ascontiguousarray(skf[:, cols])
        masks = np.zeros((4, 8, 128, 512), dtype=np.float32)
        for l in range(4):
            g = gl[l]
            cnt = KBCNT[l]
            for s in range(8):
                kb = cnt - 8 + s
                jj = kb * 128 + np.arange(128)[:, None]
                qq = 512 * g + np.arange(512)[None, :]
                masks[l, s] = (jj <= qq).astype(np.float32)
        ins.append({
            "xt": xt, "xtq": xtq, "xres": xres,
            "wq": wq_p, "wk": wk_p, "wv": wv, "wo": wo,
            "bq": bq_p, "bk": bk_p, "bv": np.ascontiguousarray(bv[None, :]),
            "bo": np.ascontiguousarray(bo[None, :]),
            "ckf": ckf, "skf": skf, "cqf": cqf, "sqf": sqf,
            "swp": swap, "id32": id32,
            "masks": masks.astype(ml_dtypes.bfloat16),
            "rgt": rgt, "wg": wg, "wu": wu, "wd": wd,
        })
    return ins


def kernel(**inputs):
    ins = _host_inputs(inputs)
    nc, em = build()
    stk = materialize(nc, em)
    try:
        res = run_bass_kernel_spmd(nc, ins, core_ids=list(range(8)))
    finally:
        stk.close()
    out = np.zeros((B, T, D), dtype=np.float32)
    for c in range(8):
        b, hf = c // 2, c % 2
        gl = G_OF_HALF[hf]
        cols = np.concatenate([np.arange(512 * g, 512 * g + 512) for g in gl])
        out[b][cols, :] = res.results[c]["out"]
    return out
